# version 07
# baseline (speedup 1.0000x reference)
import numpy as np
from contextlib import ExitStack

import concourse.bass as bass
import concourse.mybir as mybir
from concourse.bass_utils import run_bass_kernel_spmd

F32 = mybir.dt.float32
BF16 = mybir.dt.bfloat16
AF = mybir.ActivationFunctionType
ALU = mybir.AluOpType
AX = mybir.AxisListType

T = 4096
D = 1024
H = 16
DH = 64
NCORES = 8
NTT = T // 128
TQ = 1024
NQB = T // TQ
NKC = T // 128
EPS = 1e-4
EPS_LN = 1e-5
CLAMP = 12000.0
C0 = 1.0 / (CLAMP * CLAMP)
M1 = 1.5 * (2.0 ** 20)
DBG_Q2OFF = False

_swap_to = {}


def _rs(bi):
    _swap_to[bi.ins.name] = AF.Rsqrt
    return bi


def _apply_swaps(nc):
    for blk in nc.m.functions[0].blocks:
        insts = blk.instructions
        for i, inst in enumerate(insts):
            if inst.opcode == "Activation" and inst.name in _swap_to:
                clone = mybir.InstActivation(
                    name=inst.name, ins=list(inst.ins), outs=list(inst.outs),
                    func=_swap_to[inst.name])
                clone.merge_dependencies_from(inst)
                clone.sync_info = inst.sync_info
                clone.engine = inst.engine
                insts[i] = clone


ACT_P0 = 6 * NTT
ACT_PRO = 64
ACT_QB = 2 * NKC + 3
ACT_HEAD = ACT_PRO + NQB * ACT_QB
ACT_P2B = ACT_P0 + H * ACT_HEAD
ACT_TTB = ACT_P2B + 24

DVE_P0 = NTT
DVE_HEAD = NQB * (2 * NKC + 2)
DVE_P2B = DVE_P0 + H * DVE_HEAD
DVE_TTB = DVE_P2B + 16

PE_P0 = 2 * NTT
PE_PRO = 56
PE_QB = 2 * NKC + 4
PE_HEAD = PE_PRO + NQB * PE_QB
PE_P2B = PE_P0 + H * PE_HEAD


def act_probase(h):
    return ACT_P0 + h * ACT_HEAD


def act_qbbase(h, qb):
    return act_probase(h) + ACT_PRO + qb * ACT_QB


def act_sig(h, qb, kc):
    return act_qbbase(h, qb) + (1 if kc == 0 else 2 * kc)


def act_w(h, qb, kc):
    return act_qbbase(h, qb) + (2 * kc + 3 if kc < NKC - 1 else 2 * NKC)


def act_aevac(h, qb):
    return act_qbbase(h, qb) + 2 * NKC + 1


def act_ofevac(h, qb, half):
    return act_qbbase(h, qb) + 2 * NKC + 2 + half


def f_qev(nb):
    return 1 + nb


def f_kev(nb):
    return 9 + nb


def f_vgev(nb):
    return 17 + nb if nb < 4 else 21 + nb


def f_vgtrev(g):
    return 21 + g if g < 4 else 25 + g


def f_qsq(q):
    return 33 + 3 * q


def f_q2ev(nb):
    return 34 + 3 * (nb // 2) + (nb % 2)


def f_ksq(q):
    return 45 + 5 * q


def f_k2hi(nb):
    return 46 + 5 * (nb // 2) + 2 * (nb % 2)


def f_k2lo(nb):
    return 47 + 5 * (nb // 2) + 2 * (nb % 2)


def act_wdq_sq(i):
    return ACT_P2B + 3 * i + 1


def act_wdq_ev(i, half):
    return ACT_P2B + 3 * i + 2 + half


def act_ln2sq(tt):
    return ACT_TTB + 11 * tt + 1


def act_ln2sum(tt):
    return ACT_TTB + 11 * tt + 2


def act_x1(tt):
    return ACT_TTB + 11 * tt + 4


def act_xqsq(tt):
    return ACT_TTB + 11 * tt + 5


def act_xdf(tt, half):
    return ACT_TTB + 11 * tt + 6 + half


def act_ln3sq(tt):
    return ACT_TTB + 11 * tt + 8


def act_ln3sum(tt):
    return ACT_TTB + 11 * tt + 9


def act_y(tt):
    return ACT_TTB + 11 * tt + 11


def dve_headbase(h):
    return DVE_P0 + h * DVE_HEAD


def dve_sqc(h, qb, kc):
    return dve_headbase(h) + qb * (2 * NKC + 2) + (1 if kc == 0 else 2 * kc)


def dve_z(h, qb, kc):
    return dve_headbase(h) + qb * (2 * NKC + 2) + (
        2 * kc + 3 if kc < NKC - 1 else 2 * NKC)


def dve_otm(h, qb, half):
    return dve_headbase(h) + qb * (2 * NKC + 2) + 2 * NKC + half + 1


def dve_wdq_y(i):
    return DVE_P2B + 2 * i + 1


def dve_wdq_xd(i):
    return DVE_P2B + 2 * i + 2


def dve_r1(tt):
    return DVE_TTB + 6 * tt + 1


def dve_red2(tt):
    return DVE_TTB + 6 * tt + 2


def dve_x1q_y(tt):
    return DVE_TTB + 6 * tt + 3


def dve_x1q_xd(tt):
    return DVE_TTB + 6 * tt + 4


def dve_r2(tt):
    return DVE_TTB + 6 * tt + 5


def dve_red3(tt):
    return DVE_TTB + 6 * tt + 6


def tile_of(gi):
    return gi // (NQB * NKC), (gi // NKC) % NQB, gi % NKC


def build_nc(stage="full"):
    if stage.startswith("h") and stage[1:].isdigit():
        do_heads = int(stage[1:])
        do_attn = True
        do_p2 = False
    else:
        do_heads = {"p0": 0, "pro": 1, "attn1": 1, "noph2": H, "full": H}[stage]
        do_attn = stage in ("attn1", "noph2", "full")
        do_p2 = stage == "full"
    _swap_to.clear()
    nc = bass.Bass("TRN2", target_bir_lowering=False,
                   dynamic_dma_scratch_size=6144)

    x_h = nc.dram_tensor("x", [T, D], F32, kind="ExternalInput").ap()
    winT_h = nc.dram_tensor("winT", [H * 128, 8 * 256], F32,
                            kind="ExternalInput").ap()
    binq_h = nc.dram_tensor("binq", [64, H], F32, kind="ExternalInput").ap()
    bink_h = nc.dram_tensor("bink", [64, H], F32, kind="ExternalInput").ap()
    binvg_h = nc.dram_tensor("binvg", [128, H], F32, kind="ExternalInput").ap()
    woutT_h = nc.dram_tensor("woutT", [128, 8 * D], F32,
                             kind="ExternalInput").ap()
    bout_h = nc.dram_tensor("bout", [128, D], F32, kind="ExternalInput").ap()
    fw_h = nc.dram_tensor("fw", [D, D], F32, kind="ExternalInput").ap()
    fb_h = nc.dram_tensor("fb", [128, D], F32, kind="ExternalInput").ap()
    idn_h = nc.dram_tensor("idn", [128, 128], F32, kind="ExternalInput").ap()
    y_h = nc.dram_tensor("y", [T, D], F32, kind="ExternalOutput").ap()
    hN = stage.startswith("h") and stage[1:].isdigit()
    dbg = {}
    if stage != "full":
        dbg["xnf"] = nc.dram_tensor("dbg_xnf", [128, 8 * T], BF16,
                                    kind="ExternalOutput").ap()
        if stage in ("pro", "attn1", "noph2") or hN:
            dbg["qaug"] = nc.dram_tensor("dbg_qaug", [97, T], BF16,
                                         kind="ExternalOutput").ap()
            dbg["kaug"] = nc.dram_tensor("dbg_kaug", [97, T], BF16,
                                         kind="ExternalOutput").ap()
            dbg["vaug"] = nc.dram_tensor("dbg_vaug", [128, NKC * 65], BF16,
                                         kind="ExternalOutput").ap()
            dbg["gtm"] = nc.dram_tensor("dbg_gtm", [128, 2048], F32,
                                        kind="ExternalOutput").ap()
        if stage in ("attn1", "noph2") or hN:
            dbg["of"] = nc.dram_tensor("dbg_of", [128, 8 * T], BF16,
                                       kind="ExternalOutput").ap()
            dbg["vgf"] = nc.dram_tensor("dbg_vgf", [128, 2048], F32,
                                        kind="ExternalOutput").ap()
            dbg["otm"] = nc.dram_tensor("dbg_otm", [128, 1024], F32,
                                        kind="ExternalOutput").ap()
            for hh in range(do_heads):
                dbg[f"vaug{hh}"] = nc.dram_tensor(f"dbg_vaug{hh}",
                                                  [128, NKC * 65], BF16,
                                                  kind="ExternalOutput").ap()
            for nm, shp, dt in [("wt0", [128, TQ], BF16), ("wt1", [128, TQ], BF16),
                                ("atm", [128, TQ], F32), ("zt0", [128, TQ], F32),
                                ("zt1", [128, TQ], F32), ("fwq", [128, D], F32),
                                ("sig0", [128, TQ], F32), ("sig1", [128, TQ], F32)]:
                dbg[nm] = nc.dram_tensor("dbg_" + nm, shp, dt,
                                         kind="ExternalOutput").ap()

    ctx = ExitStack()
    sb = lambda name, shape, dt: ctx.enter_context(nc.sbuf_tensor(name, shape, dt))
    pst = lambda name, shape: ctx.enter_context(nc.psum_tensor(name, shape, F32))
    sem = lambda name: ctx.enter_context(nc.semaphore(name))

    with ctx:
        IDN = sb("IDN", [128, 128], F32)
        BQ = sb("BQ", [64, H], F32)
        BQm2 = sb("BQm2", [64, H], F32)
        BK = sb("BK", [64, H], F32)
        BVG = sb("BVG", [128, H], F32)
        BOB = sb("BOB", [128, D], F32)
        FB = sb("FB", [128, D], F32)
        ONESK = sb("ONESK", [64, 97], BF16)
        ONESQ = ONESK[:, 0:65]
        NEGL = sb("NEGL", [65, 97], BF16)
        STAT = sb("STAT", [128, 16], F32)
        RED2 = sb("RED2", [128, 2], F32)
        RCOL = sb("RCOL", [128, 8], F32)
        EPSC = sb("EPSC", [128, 1], F32)
        BM = sb("BM", [128, 32], F32)
        SCL = sb("SCL", [128, 32], F32)
        RSC = sb("RSC", [128, 32], F32)

        XT = [sb(f"XT{i}", [128, D], F32) for i in range(2)]
        XNF = sb("XNF", [128, 8 * T], BF16)
        OF = sb("OF", [128, 8 * T], BF16)
        WITB = sb("WITB", [128, 8 * 256], BF16)
        QAUG = sb("QAUG", [97, T], BF16)
        KAUG = sb("KAUG", [97, T], BF16)
        VGF = sb("VGF", [128, 2048], F32)
        VAUG = sb("VAUG", [128, NKC * 65], BF16)
        GTM = sb("GTM", [128, 2048], F32)

        SIG = [sb(f"SIG{i}", [128, TQ], F32) for i in range(2)]
        ATm = sb("ATm", [128, TQ], F32)
        ZT = [sb(f"ZT{i}", [128, TQ], F32) for i in range(2)]
        WTl = [sb(f"WTb{i}", [128, TQ], BF16) for i in range(2)]
        WTl.append(XT[0][:].bitcast(BF16))
        ZT.append(XT[1])

        def wtl_ap(i, a=0, b=TQ):
            return WTl[i][:, a:b] if i < 2 else WTl[2][:, a:b]
        OTM = sb("OTM", [128, 1024], F32)
        FWQ = sb("FWQ", [128, D], F32)
        X1T = [VGF[:, 0:D], VGF[:, D:2 * D]]
        YO = [GTM[:, 0:D], GTM[:, D:2 * D]]

        XNB = ZT[0]
        LNSa = XNB
        SQS = WTl[0][0:64, :]
        WOT = XNF[:, 0:8 * D]
        WDT = XNF[:, 8 * D:16 * D]
        XDF = XNF[:, 16 * D:17 * D]
        FWT = OTM
        AEV = FWQ[0:65, :]
        X2B = FWQ
        LNS2 = ATm
        QI = ATm[:].bitcast(mybir.dt.int32)
        QY = SIG[0]
        QC = SIG[1]
        QT1 = ZT[0]
        XD = ZT[1]

        PS_SQ = [pst("PS_SQ0", [128, TQ]), pst("PS_SQ1", [128, TQ])]
        PS_A = pst("PS_A", [97, TQ])
        PS_TR = [pst("PS_TR0", [128, 512]), pst("PS_TR1", [128, 512])]

        sIN = sem("sIN")
        sWL = sem("sWL")
        sOUT = sem("sOUT")
        sPE = sem("sPE")
        sACT = sem("sACT")
        sDVE = sem("sDVE")
        sQ2 = sem("sQ2")
        sQM = sem("sQM")

        in_marks = {}
        in_cnt = [0]

        def in_mark(key):
            in_cnt[0] += 16
            in_marks[key] = in_cnt[0]

        wl_marks = {}
        wl_cnt = [0]

        def wl_mark(key):
            wl_cnt[0] += 16
            wl_marks[key] = wl_cnt[0]

        pe_marks = {}
        pe_cnt = [0]

        def pe_mark(key):
            pe_cnt[0] += 1
            pe_marks[key] = pe_cnt[0]

        block = ctx.enter_context(nc.Block())

        @block.sync
        def _(sync):
            for ap_sb, ap_h in [(IDN[:], idn_h), (BQ[:], binq_h), (BK[:], bink_h),
                                (BVG[:], binvg_h), (BOB[:], bout_h), (FB[:], fb_h)]:
                sync.dma_start(out=ap_sb, in_=ap_h).then_inc(sIN, 16)
                in_cnt[0] += 16
            in_marks["consts"] = in_cnt[0]
            for i in range(NTT):
                if i >= 2:
                    sync.wait_ge(sACT, 6 * (i - 2) + 4)
                    sync.wait_ge(sDVE, i - 1)
                sync.dma_start(out=XT[i % 2][:],
                               in_=x_h[i * 128:(i + 1) * 128, :]).then_inc(sIN, 16)
                in_mark(("x0", i))
            for h in range(do_heads):
                sync.wait_ge(sACT, act_probase(h) + f_q2ev(7))
                sync.dma_start(out=QAUG[66:67, :] if DBG_Q2OFF else QAUG[65:66, :],
                               in_=QAUG[64:65, :]).then_inc(sQ2, 16)
                if f"vaug{h}" in dbg:
                    sync.wait_ge(sACT, act_probase(h) + ACT_PRO)
                    sync.dma_start(out=dbg[f"vaug{h}"], in_=VAUG[:]
                                   ).then_inc(sOUT, 16)
            if do_p2:
                sync.wait_ge(sPE, PE_P2B)
                sync.wait_ge(sWL, 16 * (H + 1))
                for i in range(8):
                    if i >= 1:
                        sync.wait_ge(sDVE, dve_wdq_y(i - 1))
                    sync.dma_start(out=FWT[:],
                                   in_=fw_h[128 * i:128 * (i + 1), :]
                                   ).then_inc(sWL, 16)
                    wl_marks[("fw", i)] = 16 * (H + 1) + 16 * (i + 1)
                sync.wait_ge(sACT, ACT_P0)
                for tt in range(NTT):
                    if tt >= 2:
                        sync.wait_ge(sDVE, dve_r1(tt - 2))
                    sync.dma_start(out=XT[tt % 2][:],
                                   in_=x_h[tt * 128:(tt + 1) * 128, :]
                                   ).then_inc(sIN, 16)
                    in_mark(("x2", tt))
                    if tt >= 1:
                        sync.wait_ge(sACT, act_y(tt - 1))
                        sync.dma_start(out=y_h[(tt - 1) * 128:tt * 128, :],
                                       in_=YO[(tt - 1) % 2][:]).then_inc(sOUT, 16)
                sync.wait_ge(sACT, act_y(NTT - 1))
                sync.dma_start(out=y_h[(NTT - 1) * 128:NTT * 128, :],
                               in_=YO[(NTT - 1) % 2][:]).then_inc(sOUT, 16)
                sync.wait_ge(sOUT, 16 * NTT)
            if stage != "full":
                if stage == "p0":
                    sync.wait_ge(sACT, ACT_P0)
                elif stage == "pro":
                    sync.wait_ge(sACT, act_probase(0) + ACT_PRO)
                else:
                    sync.wait_ge(sACT, act_probase(do_heads - 1) + ACT_HEAD)
                nst = 16 * sum(1 for k in dbg if k.startswith("vaug") and k != "vaug")
                for key, hbm in dbg.items():
                    if key.startswith("vaug") and key != "vaug":
                        continue
                    src = {"xnf": XNF[:], "qaug": QAUG[:], "kaug": KAUG[:],
                           "vaug": VAUG[:], "gtm": GTM[:], "of": OF[:],
                           "vgf": VGF[:], "otm": OTM[:], "wt0": WTl[0][:],
                           "wt1": WTl[1][:], "atm": ATm[:], "zt0": ZT[0][:],
                           "zt1": ZT[1][:], "fwq": FWQ[:], "sig0": SIG[0][:],
                           "sig1": SIG[1][:]}[key]
                    sync.dma_start(out=hbm, in_=src).then_inc(sOUT, 16)
                    nst += 16
                sync.dma_start(out=y_h[0:128, :], in_=XT[0][:]).then_inc(sOUT, 16)
                nst += 16
                sync.wait_ge(sOUT, nst)

        @block.gpsimd
        def _(gpsimd):
            for h in range(do_heads):
                if h >= 1:
                    gpsimd.wait_ge(sPE, PE_P0 + (h - 1) * PE_HEAD + 28)
                gpsimd.dma_start(out=WITB[:],
                                 in_=winT_h[128 * h:128 * (h + 1), :]
                                 ).then_inc(sWL, 16)
                wl_mark(("wit", h))
            if not do_p2:
                return
            gpsimd.wait_ge(sPE, PE_P0 + (H - 1) * PE_HEAD + 28)
            gpsimd.dma_start(out=WOT, in_=woutT_h).then_inc(sWL, 16)
            wl_mark("wot")

        def emit_wv(tensor, lh, lqb, lkc, ab):
            gi = (lh * NQB + lqb) * NKC + lkc
            tensor.wait_ge(sACT, act_w(lh, lqb, lkc))
            if lqb == 0:
                tensor.wait_ge(sACT, ab + f_vgtrev(lkc // 4))
            if lkc == 0 and (lh * NQB + lqb) > 0:
                pq = lh * NQB + lqb - 1
                tensor.wait_ge(sACT, act_aevac(pq // NQB, pq % NQB))
            for half in range(2):
                last = tensor.matmul(PS_A[0:65, half * 512:(half + 1) * 512],
                                     VAUG[:, lkc * 65:(lkc + 1) * 65],
                                     wtl_ap(gi % 3, half * 512,
                                            (half + 1) * 512),
                                     start=(lkc == 0), stop=(lkc == NKC - 1))
            last.then_inc(sPE, 1)
            pe_mark(("wv", lh, lqb, lkc))

        @block.tensor
        def _(tensor):
            tensor.wait_ge(sIN, in_marks["consts"])
            for i in range(NTT):
                tensor.wait_ge(sACT, 6 * i + 4)
                for half in range(2):
                    if i > 0:
                        tensor.wait_ge(sACT, 6 * (i - 1) + 5 + half)
                    bank = PS_TR[half]
                    for j in range(4):
                        c = half * 4 + j
                        last = tensor.transpose(bank[:, j * 128:(j + 1) * 128],
                                                XNB[:, c * 128:(c + 1) * 128],
                                                IDN[:])
                    last.then_inc(sPE, 1)
                    pe_mark(("xntr", i, half))

            for h in range(do_heads):
                ab = act_probase(h)
                tensor.wait_ge(sWL, wl_marks[("wit", h)])
                if h == 0:
                    tensor.wait_ge(sACT, ACT_P0)
                for nb in range(8):
                    bank = PS_SQ[nb % 2]
                    if h > 0 and nb < 2:
                        tensor.wait_ge(sDVE, dve_sqc(h - 1, NQB - 1, NKC - 1))
                    if nb >= 2:
                        tensor.wait_ge(sACT, ab + f_qev(nb - 2))
                    for c in range(8):
                        last = tensor.matmul(
                            bank[0:64, 0:512],
                            WITB[:, c * 256:c * 256 + 64],
                            XNF[:, c * T + nb * 512:c * T + (nb + 1) * 512],
                            start=(c == 0), stop=(c == 7))
                    last.then_inc(sPE, 1)
                    pe_mark(("qproj", h, nb))
                for nb in range(8):
                    bank = PS_SQ[nb % 2]
                    tensor.wait_ge(sACT,
                                   ab + (f_qev(6 + nb) if nb < 2 else f_kev(nb - 2)))
                    for c in range(8):
                        last = tensor.matmul(
                            bank[0:64, 0:512],
                            WITB[:, c * 256 + 64:c * 256 + 128],
                            XNF[:, c * T + nb * 512:c * T + (nb + 1) * 512],
                            start=(c == 0), stop=(c == 7))
                    last.then_inc(sPE, 1)
                    pe_mark(("kproj", h, nb))
                for nb in range(4):
                    bank = PS_SQ[nb % 2]
                    tensor.wait_ge(sACT,
                                   ab + (f_kev(6 + nb) if nb < 2 else f_vgev(nb - 2)))
                    for c in range(8):
                        last = tensor.matmul(
                            bank[:, 0:512],
                            WITB[:, c * 256 + 128:c * 256 + 256],
                            XNF[:, c * T + nb * 512:c * T + (nb + 1) * 512],
                            start=(c == 0), stop=(c == 7))
                    last.then_inc(sPE, 1)
                    pe_mark(("vgproj", h, nb))
                for g in range(4):
                    bank = PS_TR[g % 2]
                    tensor.wait_ge(sACT, ab + f_vgev(g))
                    if g >= 2:
                        tensor.wait_ge(sACT, ab + f_vgtrev(g - 2))
                    elif h > 0:
                        tensor.wait_ge(sACT, act_ofevac(h - 1, NQB - 1, g))
                    for j in range(4):
                        kc = g * 4 + j
                        last = tensor.transpose(
                            bank[:, j * 128:(j + 1) * 128],
                            VGF[:, (kc % 16) * 128:(kc % 16 + 1) * 128],
                            IDN[:])
                    last.then_inc(sPE, 1)
                    pe_mark(("vgtr", h, g))
                for nb in range(4, 8):
                    bank = PS_SQ[nb % 2]
                    tensor.wait_ge(sACT, ab + f_vgev(nb - 2))
                    for c in range(8):
                        last = tensor.matmul(
                            bank[:, 0:512],
                            WITB[:, c * 256 + 128:c * 256 + 256],
                            XNF[:, c * T + nb * 512:c * T + (nb + 1) * 512],
                            start=(c == 0), stop=(c == 7))
                    last.then_inc(sPE, 1)
                    pe_mark(("vgproj", h, nb))
                for g in range(4, 8):
                    bank = PS_TR[g % 2]
                    tensor.wait_ge(sACT, ab + f_vgev(g))
                    tensor.wait_ge(sACT, ab + f_vgtrev(g - 2))
                    for j in range(4):
                        kc = g * 4 + j
                        last = tensor.transpose(
                            bank[:, j * 128:(j + 1) * 128],
                            VGF[:, (kc % 16) * 128:(kc % 16 + 1) * 128],
                            IDN[:])
                    last.then_inc(sPE, 1)
                    pe_mark(("vgtr", h, g))
                for nb in range(8):
                    tensor.wait_ge(sACT, ab + f_qsq(nb // 2))
                    if nb == 0 and h > 0:
                        tensor.wait_ge(sACT, act_aevac(h - 1, NQB - 1))
                    if nb >= 1:
                        tensor.wait_ge(sACT, ab + f_q2ev(nb - 1))
                    tensor.matmul(PS_A[0:65, 0:512], ONESQ,
                                  SQS[:, (nb % 2) * 512:(nb % 2 + 1) * 512],
                                  start=True, stop=True).then_inc(sPE, 1)
                    pe_mark(("q2", h, nb))
                for nb in range(8):
                    tensor.wait_ge(sACT, ab + f_ksq(nb // 2))
                    if nb == 0:
                        tensor.wait_ge(sACT, ab + f_q2ev(7))
                    if nb >= 1:
                        tensor.wait_ge(sACT, ab + f_k2lo(nb - 1))
                    tensor.matmul(PS_A[0:97, 0:512], ONESK[:],
                                  SQS[:, (nb % 2) * 512:(nb % 2 + 1) * 512],
                                  start=True, stop=False).then_inc(sPE, 1)
                    pe_mark(("k2", h, nb))
                    tensor.wait_ge(sACT, ab + f_k2hi(nb))
                    tensor.matmul(PS_A[0:97, 0:512], NEGL[64:65, :],
                                  KAUG[64:65, nb * 512:(nb + 1) * 512],
                                  start=False, stop=True).then_inc(sPE, 1)
                    pe_mark(("k2b", h, nb))

                for qb in range(NQB if do_attn else 0):
                    for kc in range(NKC):
                        gi = (h * NQB + qb) * NKC + kc
                        buf = gi % 2
                        bank = PS_SQ[buf]
                        if qb == 0 and kc < 2:
                            tensor.wait_ge(sACT, ab + f_vgev(6 + kc))
                        else:
                            ph, pqb, pkc = tile_of(gi - 2)
                            tensor.wait_ge(sDVE, dve_sqc(ph, pqb, pkc))
                        if qb == 0 and kc == 0:
                            tensor.wait_ge(sACT, ab + f_k2lo(7))
                            tensor.wait_ge(sQM, h + 1)
                        for half in range(2):
                            last = tensor.matmul(
                                bank[:, half * 512:(half + 1) * 512],
                                KAUG[:, kc * 128:(kc + 1) * 128],
                                QAUG[:, qb * TQ + half * 512:qb * TQ + half * 512 + 512],
                                start=True, stop=True)
                        last.then_inc(sPE, 1)
                        pe_mark(("sq", h, qb, kc))
                        if kc >= 1:
                            emit_wv(tensor, h, qb, kc - 1, ab)
                    emit_wv(tensor, h, qb, NKC - 1, ab)
                    tensor.wait_ge(sACT, act_aevac(h, qb))
                    for half in range(2):
                        bank = PS_TR[half]
                        if qb == 0:
                            tensor.wait_ge(sACT, ab + f_vgtrev(6 + half))
                        else:
                            tensor.wait_ge(sACT, act_ofevac(h, qb - 1, half))
                        for j in range(4):
                            g = half * 4 + j
                            last = tensor.transpose(
                                bank[:, j * 130:j * 130 + 65],
                                AEV[:, g * 128:(g + 1) * 128],
                                IDN[0:65, 0:65])
                        last.then_inc(sPE, 1)
                        pe_mark(("atr", h, qb, half))
                    for half in range(2):
                        bank = PS_TR[half]
                        tensor.wait_ge(sDVE, dve_otm(h, qb, half))
                        for j in range(4):
                            g = half * 4 + j
                            last = tensor.transpose(
                                bank[:, j * 128:(j + 1) * 128],
                                OTM[:, g * 128:(g + 1) * 128], IDN[:])
                        last.then_inc(sPE, 1)
                        pe_mark(("otr", h, qb, half))

            if not do_p2:
                return
            for i in range(8):
                for half in range(2):
                    bank = PS_TR[half]
                    tensor.wait_ge(sDVE, dve_wdq_xd(i))
                    if i == 0:
                        tensor.wait_ge(sACT, act_ofevac(H - 1, NQB - 1, half))
                    else:
                        tensor.wait_ge(sACT, act_wdq_ev(i - 1, half))
                    for j in range(4):
                        c = half * 4 + j
                        last = tensor.transpose(
                            bank[:, j * 128:(j + 1) * 128],
                            FWQ[:, c * 128:(c + 1) * 128], IDN[:])
                    last.then_inc(sPE, 1)
                    pe_mark(("wdtr", i, half))
            tensor.wait_ge(sWL, wl_marks["wot"])
            tensor.wait_ge(sACT, ACT_P2B)
            for tt in range(NTT):
                if tt >= 1:
                    tensor.wait_ge(sDVE, dve_r1(tt - 1))
                for c in range(8):
                    for half in range(2):
                        last = tensor.matmul(
                            PS_SQ[0][:, half * 512:(half + 1) * 512],
                            OF[:, c * T + tt * 128:c * T + (tt + 1) * 128],
                            WOT[:, c * D + half * 512:c * D + (half + 1) * 512],
                            start=(c == 0), stop=(c == 7))
                last.then_inc(sPE, 1)
                pe_mark(("z", tt))
                for half in range(2):
                    bank = PS_TR[half]
                    tensor.wait_ge(sDVE, dve_x1q_xd(tt))
                    if tt == 0:
                        tensor.wait_ge(sACT, act_wdq_ev(7, half))
                    else:
                        tensor.wait_ge(sACT, act_xdf(tt - 1, half))
                    for j in range(4):
                        c = half * 4 + j
                        last = tensor.transpose(
                            bank[:, j * 128:(j + 1) * 128],
                            XD[:, c * 128:(c + 1) * 128], IDN[:])
                    last.then_inc(sPE, 1)
                    pe_mark(("xdtr", tt, half))
                tensor.wait_ge(sACT, act_xdf(tt, 1))
                if tt >= 1:
                    tensor.wait_ge(sDVE, dve_r2(tt - 1))
                for c in range(8):
                    for half in range(2):
                        last = tensor.matmul(
                            PS_SQ[1][:, half * 512:(half + 1) * 512],
                            XDF[:, c * 128:(c + 1) * 128],
                            WDT[:, c * D + half * 512:c * D + (half + 1) * 512],
                            start=(c == 0), stop=(c == 7))
                last.then_inc(sPE, 1)
                pe_mark(("ffn", tt))

        def ln_stats(scalar, red2):
            scalar.activation(STAT[:, 0:1], red2[:, 0:1], AF.Prelu, alpha=1.0,
                              bias=red2[:, 1:2], scale=1.0).then_inc(sACT, 1)
            scalar.activation(STAT[:, 7:8], STAT[:, 1:2], AF.Prelu, alpha=1.0,
                              scale=1.0 / D)
            scalar.activation(STAT[:, 6:7], STAT[:, 0:1], AF.Prelu, alpha=1.0,
                              scale=-1.0 / D)
            scalar.activation(STAT[:, 0:1], STAT[:, 0:1], AF.Prelu, alpha=1.0,
                              scale=1.0 / D)
            scalar.activation(STAT[:, 5:6], STAT[:, 0:1], AF.Prelu, alpha=1.0,
                              scale=STAT[:, 6:7])
            scalar.activation(STAT[:, 2:3], STAT[:, 5:6], AF.Prelu, alpha=1.0,
                              bias=STAT[:, 7:8], scale=1.0)
            _rs(scalar.activation(STAT[:, 3:4], STAT[:, 2:3], AF.Square,
                                  bias=EPSC[:, 0:1], scale=1.0))
            scalar.activation(STAT[:, 4:5], STAT[:, 3:4], AF.Prelu, alpha=1.0,
                              scale=STAT[:, 6:7])
            scalar.activation(STAT[:, 8:9], EPSC[:, 0:1], AF.Prelu, alpha=1.0,
                              scale=1.0).then_inc(sACT, 1)

        @block.scalar
        def _(scalar):
            for i in range(NTT):
                scalar.wait_ge(sIN, in_marks[("x0", i)])
                if i >= 1:
                    scalar.wait_ge(sPE, pe_marks[("xntr", i - 1, 1)])
                xt = XT[i % 2]
                scalar.activation(LNSa[:], xt[:], AF.Square,
                                  accum_out=STAT[:, 1:2]).then_inc(sACT, 1)
                scalar.wait_ge(sDVE, i + 1)
                ln_stats(scalar, RED2)
                scalar.activation(XNB[:], xt[:], AF.Prelu, alpha=1.0,
                                  bias=STAT[:, 4:5], scale=STAT[:, 3:4]
                                  ).then_inc(sACT, 1)
                for half in range(2):
                    scalar.wait_ge(sPE, pe_marks[("xntr", i, half)])
                    src = PS_TR[half][:].rearrange("p (j f) -> p j f", f=128)
                    dst = XNF[:].rearrange("p (c f) -> p c f", f=T)[
                        :, half * 4:(half + 1) * 4, i * 128:(i + 1) * 128]
                    scalar.activation(dst, src, AF.Copy).then_inc(sACT, 1)

            for h in range(do_heads):
                ab = act_probase(h)
                if h == 0:
                    scalar.wait_ge(sDVE, DVE_P0)
                for nb in range(8):
                    scalar.wait_ge(sPE, pe_marks[("qproj", h, nb)])
                    bank = PS_SQ[nb % 2]
                    scalar.activation(QAUG[0:64, nb * 512:(nb + 1) * 512],
                                      bank[0:64, 0:512], AF.Prelu, alpha=1.0,
                                      bias=BQm2[:, h:h + 1], scale=-2.0
                                      ).then_inc(sACT, 1)
                for nb in range(8):
                    scalar.wait_ge(sPE, pe_marks[("kproj", h, nb)])
                    bank = PS_SQ[nb % 2]
                    scalar.activation(KAUG[0:64, nb * 512:(nb + 1) * 512],
                                      bank[0:64, 0:512], AF.Prelu, alpha=1.0,
                                      bias=BK[:, h:h + 1], scale=1.0
                                      ).then_inc(sACT, 1)
                for nb in range(8):
                    scalar.wait_ge(sPE, pe_marks[("vgproj", h, nb)])
                    if nb >= 4:
                        scalar.wait_ge(sPE, pe_marks[("vgtr", h, nb - 4)])
                    bank = PS_SQ[nb % 2]
                    scalar.activation(VGF[:, (nb % 4) * 512:(nb % 4 + 1) * 512],
                                      bank[:, 0:512], AF.Prelu, alpha=1.0,
                                      bias=BVG[:, h:h + 1], scale=1.0
                                      ).then_inc(sACT, 1)
                    if nb == 3:
                        for g in range(4):
                            scalar.wait_ge(sPE, pe_marks[("vgtr", h, g)])
                            if g < 2 and h > 0:
                                scalar.wait_ge(
                                    sPE, pe_marks[("wv", h - 1, NQB - 1, NKC - 1)])
                                scalar.wait_ge(sDVE, dve_otm(h - 1, NQB - 1, 1))
                            bank2 = PS_TR[g % 2]
                            src = bank2[:].rearrange("p (j f) -> p j f", f=128)
                            vdst = VAUG[:].rearrange("p (k f) -> p k f", f=65)[
                                :, g * 4:(g + 1) * 4, 0:64]
                            scalar.activation(vdst, src[:, :, 0:64], AF.Copy)
                            gdst = GTM[:].rearrange("p (k f) -> p k f", f=DH)[
                                :, g * 4:(g + 1) * 4, :]
                            scalar.activation(gdst, src[:, :, 64:128], AF.Copy
                                              ).then_inc(sACT, 1)
                for g in range(4, 8):
                    scalar.wait_ge(sPE, pe_marks[("vgtr", h, g)])
                    bank2 = PS_TR[g % 2]
                    src = bank2[:].rearrange("p (j f) -> p j f", f=128)
                    vdst = VAUG[:].rearrange("p (k f) -> p k f", f=65)[
                        :, g * 4:(g + 1) * 4, 0:64]
                    scalar.activation(vdst, src[:, :, 0:64], AF.Copy)
                    gdst = GTM[:].rearrange("p (k f) -> p k f", f=DH)[
                        :, g * 4:(g + 1) * 4, :]
                    scalar.activation(gdst, src[:, :, 64:128], AF.Copy
                                      ).then_inc(sACT, 1)
                for q in range(4):
                    if q == 0:
                        if h > 0:
                            lg = h * 128 - 1
                            while lg % 3:
                                lg -= 1
                            scalar.wait_ge(sPE, pe_marks[("wv", *tile_of(lg))])
                        else:
                            scalar.wait_ge(sPE, pe_marks[("vgtr", h, 7)])
                    else:
                        scalar.wait_ge(sPE, pe_marks[("q2", h, 2 * q - 1)])
                    scalar.activation(SQS[:], QAUG[0:64, q * 1024:(q + 1) * 1024],
                                      AF.Square, scale=0.5).then_inc(sACT, 1)
                    for half in range(2):
                        nb = 2 * q + half
                        scalar.wait_ge(sPE, pe_marks[("q2", h, nb)])
                        if q == 0 and half == 0 and h > 0:
                            scalar.wait_ge(sQ2, 16 * h)
                        scalar.activation(QAUG[64:65, nb * 512:(nb + 1) * 512],
                                          PS_A[64:65, 0:512], AF.Copy
                                          ).then_inc(sACT, 1)
                for q in range(4):
                    scalar.wait_ge(sPE, pe_marks[("q2", h, 7)] if q == 0
                                   else pe_marks[("k2b", h, 2 * q - 1)])
                    scalar.activation(SQS[:], KAUG[0:64, q * 1024:(q + 1) * 1024],
                                      AF.Square, scale=1.0).then_inc(sACT, 1)
                    for half in range(2):
                        nb = 2 * q + half
                        scalar.wait_ge(sPE, pe_marks[("k2", h, nb)])
                        scalar.activation(KAUG[64:65, nb * 512:(nb + 1) * 512],
                                          PS_A[64:65, 0:512], AF.Copy
                                          ).then_inc(sACT, 1)
                        scalar.wait_ge(sPE, pe_marks[("k2b", h, nb)])
                        scalar.activation(KAUG[96:97, nb * 512:(nb + 1) * 512],
                                          PS_A[96:97, 0:512], AF.Copy
                                          ).then_inc(sACT, 1)

                for qb in range(NQB if do_attn else 0):
                    def emit_sig(kc):
                        gi = (h * NQB + qb) * NKC + kc
                        scalar.wait_ge(sDVE, dve_sqc(h, qb, kc))
                        _rs(scalar.activation(SIG[gi % 2][:], SIG[gi % 2][:],
                                              AF.Square)).then_inc(sACT, 1)

                    def emit_W(kc):
                        gi = (h * NQB + qb) * NKC + kc
                        scalar.wait_ge(sDVE, dve_z(h, qb, kc))
                        if gi >= 3:
                            ph, pqb, pkc = tile_of(gi - 3)
                            scalar.wait_ge(sPE, pe_marks[("wv", ph, pqb, pkc)])
                        scalar.activation(wtl_ap(gi % 3), ZT[gi % 3][:],
                                          AF.Square,
                                          scale=EPS).then_inc(sACT, 1)

                    emit_sig(0)
                    for kc in range(NKC - 1):
                        emit_sig(kc + 1)
                        emit_W(kc)
                    emit_W(NKC - 1)
                    scalar.wait_ge(sPE, pe_marks[("wv", h, qb, NKC - 1)])
                    if h * NQB + qb > 0:
                        pq = h * NQB + qb - 1
                        scalar.wait_ge(sPE, pe_marks[("atr", pq // NQB, pq % NQB, 1)])
                    scalar.activation(AEV[:], PS_A[0:65, :], AF.Copy).then_inc(sACT, 1)
                    for half in range(2):
                        scalar.wait_ge(sPE, pe_marks[("otr", h, qb, half)])
                        bank = PS_TR[half]
                        part = (h % 2) * 64
                        base = (h // 2) * T + qb * TQ + half * 512
                        src = bank[part:part + 64, :].rearrange(
                            "p (g f) -> p g f", f=128)
                        dst = OF[part:part + 64, base:base + 512].rearrange(
                            "p (g f) -> p g f", f=128)
                        scalar.activation(dst, src, AF.Copy).then_inc(sACT, 1)

            if not do_p2:
                return
            for i in range(8):
                scalar.wait_ge(sDVE, dve_wdq_y(i))
                scalar.activation(LNS2[:], QY[:], AF.Square).then_inc(sACT, 1)
                for half in range(2):
                    scalar.wait_ge(sPE, pe_marks[("wdtr", i, half)])
                    bank = PS_TR[half]
                    src = bank[:].rearrange("p (j f) -> p j f", f=128)
                    dst = WDT.rearrange("p (c f) -> p c f", f=D)[
                        :, half * 4:(half + 1) * 4, i * 128:(i + 1) * 128]
                    scalar.activation(dst, src, AF.Copy).then_inc(sACT, 1)
            for tt in range(NTT):
                R = YO[tt % 2]
                scalar.wait_ge(sDVE, dve_r1(tt))
                scalar.activation(LNS2[:], R[:], AF.Square,
                                  accum_out=STAT[:, 1:2]).then_inc(sACT, 1)
                scalar.wait_ge(sDVE, dve_red2(tt))
                ln_stats(scalar, RED2)
                scalar.activation(X1T[tt % 2][:], R[:], AF.Prelu, alpha=1.0,
                                  bias=STAT[:, 4:5], scale=STAT[:, 3:4]
                                  ).then_inc(sACT, 1)
                scalar.wait_ge(sDVE, dve_x1q_y(tt))
                scalar.activation(LNS2[:], QY[:], AF.Square).then_inc(sACT, 1)
                for half in range(2):
                    scalar.wait_ge(sPE, pe_marks[("xdtr", tt, half)])
                    bank = PS_TR[half]
                    src = bank[:].rearrange("p (j f) -> p j f", f=128)
                    dst = XDF.rearrange("p (c f) -> p c f", f=128)[
                        :, half * 4:(half + 1) * 4, :]
                    scalar.activation(dst, src, AF.Copy).then_inc(sACT, 1)
                scalar.wait_ge(sDVE, dve_r2(tt))
                scalar.activation(LNS2[:], R[:], AF.Square,
                                  accum_out=STAT[:, 1:2]).then_inc(sACT, 1)
                scalar.wait_ge(sDVE, dve_red3(tt))
                ln_stats(scalar, RED2)
                scalar.activation(R[:], R[:], AF.Prelu, alpha=1.0, bias=STAT[:, 4:5],
                                  scale=STAT[:, 3:4]).then_inc(sACT, 1)

        @block.vector
        def _(vector):
            vector.wait_ge(sIN, in_marks["consts"])
            vector.tensor_scalar(out=BQm2[:], in0=BQ[:], scalar1=-2.0,
                                 scalar2=None, op0=ALU.mult)
            vector.memset(KAUG[64:96, :], 0.0)
            vector.memset(KAUG[64:66, :], 1.0)
            vector.memset(KAUG[96:97, :], 0.0)
            vector.memset(QAUG[64:96, :], 1.0)
            vector.memset(QAUG[96:97, :], 1.0)
            vector.memset(VAUG[:].rearrange("p (k f) -> p k f", f=65)[:, :, 64:65],
                          1.0)
            vector.memset(EPSC[:], EPS_LN)
            vector.memset(ONESK[:], 0.0)
            vector.memset(ONESK[:, 64:65], 1.0)
            vector.memset(ONESK[:, 96:97], 1.0)
            vector.memset(NEGL[64:65, :], 0.0)
            vector.memset(NEGL[64:65, 96:97], -1.0)

            for i in range(NTT):
                vector.wait_ge(sIN, in_marks[("x0", i)])
                if i >= 1:
                    vector.wait_ge(sACT, 6 * (i - 1) + 2)
                xt = XT[i % 2]
                vector.tensor_reduce(out=RED2[:],
                                     in_=xt[:].rearrange("p (b k) -> p b k",
                                                         k=512),
                                     axis=AX.X, op=ALU.add).then_inc(sDVE, 1)

            for h in range(do_heads):
                ab = act_probase(h)
                if do_attn:
                    vector.wait_ge(sQ2, 16 * (h + 1))
                    vector.memset(QAUG[64:65, :], 1.0).then_inc(sQM, 1)
                for qb in range(NQB if do_attn else 0):
                    def emit_sqc(kc):
                        gi = (h * NQB + qb) * NKC + kc
                        vector.wait_ge(sPE, pe_marks[("sq", h, qb, kc)])
                        vector.tensor_scalar(out=SIG[gi % 2][:],
                                             in0=PS_SQ[gi % 2][:],
                                             scalar1=C0, scalar2=None,
                                             op0=ALU.max).then_inc(sDVE, 1)

                    def emit_z(kc):
                        gi = (h * NQB + qb) * NKC + kc
                        vector.wait_ge(sACT, act_sig(h, qb, kc))
                        if gi >= 3:
                            ph, pqb, pkc = tile_of(gi - 3)
                            vector.wait_ge(sACT, act_w(ph, pqb, pkc))
                        vector.scalar_tensor_tensor(
                            out=ZT[gi % 3][:], in0=SIG[gi % 2][:],
                            scalar=-1.0 / EPS, in1=SIG[gi % 2][:],
                            op0=ALU.add, op1=ALU.mult).then_inc(sDVE, 1)

                    emit_sqc(0)
                    for kc in range(NKC - 1):
                        emit_sqc(kc + 1)
                        emit_z(kc)
                    emit_z(NKC - 1)
                    for half in range(2):
                        vector.wait_ge(sPE, pe_marks[("atr", h, qb, half)])
                        bank = PS_TR[half]
                        for j in range(4):
                            vector.tensor_scalar(
                                out=RCOL[:, j:j + 1],
                                in0=bank[:, j * 130 + 64:j * 130 + 65],
                                scalar1=EPS, scalar2=None, op0=ALU.add)
                        for j in range(4):
                            vector.reciprocal(out=RCOL[:, 4 + j:5 + j],
                                              in_=RCOL[:, j:j + 1])
                        last = None
                        for j in range(4):
                            g = half * 4 + j
                            at = bank[:, j * 130:j * 130 + 65]
                            kcg = qb * 8 + g
                            col = g * 128 + (h % 2) * 64
                            last = vector.scalar_tensor_tensor(
                                out=OTM[:, col:col + 64],
                                in0=at[:, 0:64], scalar=RCOL[:, 4 + j:5 + j],
                                in1=GTM[:, kcg * DH:(kcg + 1) * DH],
                                op0=ALU.mult, op1=ALU.mult)
                        last.then_inc(sDVE, 1)

            def quant(xsrc, xdst, gate_sq, gate_prev_sq, xd_guard):
                hs = [slice(0, 512), slice(512, 1024)]
                bs = [slice(0, 16), slice(16, 32)]
                x3h = [xsrc[:, s].rearrange("p (b k) -> p b k", k=32) for s in hs]
                for hf in range(2):
                    vector.tensor_reduce(out=BM[:, bs[hf]], in_=x3h[hf],
                                         axis=AX.X, op=ALU.max,
                                         apply_absolute_value=True)
                for hf in range(2):
                    vector.tensor_scalar(out=SCL[:, bs[hf]], in0=BM[:, bs[hf]],
                                         scalar1=1.0 / 7.5, scalar2=1e-30,
                                         op0=ALU.mult, op1=ALU.max)
                for hf in range(2):
                    vector.reciprocal(out=RSC[:, bs[hf]], in_=SCL[:, bs[hf]])
                if gate_prev_sq is not None:
                    vector.wait_ge(sACT, gate_prev_sq)
                last = None
                for hf in range(2):
                    last = vector.tensor_tensor(
                        out=QY[:, hs[hf]].rearrange("p (b k) -> p b k", k=32),
                        in0=x3h[hf],
                        in1=RSC[:, bs[hf]].unsqueeze(2).broadcast_to(
                            [128, 16, 32]),
                        op=ALU.mult)
                last.then_inc(sDVE, 1)
                vector.wait_ge(sACT, gate_sq)
                for hf in range(2):
                    vector.tensor_scalar(out=QT1[:, hs[hf]], in0=LNS2[:, hs[hf]],
                                         scalar1=4.0, scalar2=None,
                                         op0=ALU.is_ge)
                for hf in range(2):
                    vector.tensor_scalar(out=QC[:, hs[hf]], in0=LNS2[:, hs[hf]],
                                         scalar1=16.0, scalar2=None,
                                         op0=ALU.is_ge)
                for hf in range(2):
                    vector.tensor_scalar(out=QT1[:, hs[hf]], in0=QT1[:, hs[hf]],
                                         scalar1=-0.5, scalar2=1.0,
                                         op0=ALU.mult, op1=ALU.add)
                for hf in range(2):
                    vector.tensor_scalar(out=QC[:, hs[hf]], in0=QC[:, hs[hf]],
                                         scalar1=-0.5, scalar2=1.0,
                                         op0=ALU.mult, op1=ALU.add)
                for hf in range(2):
                    vector.scalar_tensor_tensor(out=QC[:, hs[hf]],
                                                in0=QC[:, hs[hf]], scalar=8.0,
                                                in1=QT1[:, hs[hf]],
                                                op0=ALU.mult, op1=ALU.mult)
                for hf in range(2):
                    vector.tensor_tensor(out=QT1[:, hs[hf]], in0=QY[:, hs[hf]],
                                         in1=QC[:, hs[hf]], op=ALU.mult)
                for hf in range(2):
                    vector.tensor_copy(out=QI[:, hs[hf]], in_=QT1[:, hs[hf]])
                for hf in range(2):
                    vector.reciprocal(out=QY[:, hs[hf]], in_=QC[:, hs[hf]])
                for hf in range(2):
                    vector.tensor_copy(out=QT1[:, hs[hf]], in_=QI[:, hs[hf]])
                for hf in range(2):
                    vector.tensor_tensor(out=QT1[:, hs[hf]], in0=QT1[:, hs[hf]],
                                         in1=QY[:, hs[hf]], op=ALU.mult)
                if xd_guard is not None:
                    vector.wait_ge(sPE, xd_guard)
                last = None
                for hf in range(2):
                    last = vector.tensor_tensor(
                        out=xdst[:, hs[hf]].rearrange("p (b k) -> p b k", k=32),
                        in0=QT1[:, hs[hf]].rearrange("p (b k) -> p b k", k=32),
                        in1=SCL[:, bs[hf]].unsqueeze(2).broadcast_to(
                            [128, 16, 32]),
                        op=ALU.mult)
                last.then_inc(sDVE, 1)

            if not do_p2:
                return
            for i in range(8):
                vector.wait_ge(sWL, wl_marks[("fw", i)])
                quant(FWT[:], FWQ[:], act_wdq_sq(i),
                      act_wdq_sq(i - 1) if i > 0 else None,
                      pe_marks[("wdtr", i - 1, 1)] if i > 0 else
                      pe_marks[("atr", H - 1, NQB - 1, 1)])

            hs2 = [slice(0, 512), slice(512, 1024)]
            for tt in range(NTT):
                vector.wait_ge(sIN, in_marks[("x2", tt)])
                if tt == 0:
                    vector.wait_ge(sPE, pe_marks[("wdtr", 7, 1)])
                for hf in range(2):
                    s = hs2[hf]
                    vector.scalar_tensor_tensor(out=X2B[:, s],
                                                in0=XT[tt % 2][:, s],
                                                scalar=2.0, in1=BOB[:, s],
                                                op0=ALU.mult, op1=ALU.add)
                vector.wait_ge(sPE, pe_marks[("z", tt)])
                if tt >= 2:
                    vector.wait_ge(sOUT, 16 * (tt - 1))
                last = None
                for hf in range(2):
                    s = hs2[hf]
                    last = vector.tensor_tensor(out=YO[tt % 2][:, s],
                                                in0=X2B[:, s],
                                                in1=PS_SQ[0][:, s], op=ALU.add)
                last.then_inc(sDVE, 1)
                if tt >= 1:
                    vector.wait_ge(sACT, act_ln3sum(tt - 1))
                vector.tensor_reduce(out=RED2[:, 0:1],
                                     in_=YO[tt % 2][:, 0:512].rearrange(
                                         "p (b k) -> p b k", k=512),
                                     axis=AX.X, op=ALU.add)
                vector.tensor_reduce(out=RED2[:, 1:2],
                                     in_=YO[tt % 2][:, 512:1024].rearrange(
                                         "p (b k) -> p b k", k=512),
                                     axis=AX.X, op=ALU.add).then_inc(sDVE, 1)
                vector.wait_ge(sACT, act_x1(tt))
                quant(X1T[tt % 2], XD[:], act_xqsq(tt),
                      act_xqsq(tt - 1) if tt > 0 else act_wdq_sq(7),
                      pe_marks[("xdtr", tt - 1, 1)] if tt > 0 else
                      pe_marks[("wdtr", 7, 1)])
                for hf in range(2):
                    s = hs2[hf]
                    vector.tensor_tensor(out=X2B[:, s], in0=X1T[tt % 2][:, s],
                                         in1=FB[:, s], op=ALU.add)
                vector.wait_ge(sPE, pe_marks[("ffn", tt)])
                last = None
                for hf in range(2):
                    s = hs2[hf]
                    last = vector.tensor_tensor(out=YO[tt % 2][:, s],
                                                in0=X2B[:, s],
                                                in1=PS_SQ[1][:, s], op=ALU.add)
                last.then_inc(sDVE, 1)
                vector.wait_ge(sACT, act_ln2sum(tt))
                vector.tensor_reduce(out=RED2[:, 0:1],
                                     in_=YO[tt % 2][:, 0:512].rearrange(
                                         "p (b k) -> p b k", k=512),
                                     axis=AX.X, op=ALU.add)
                vector.tensor_reduce(out=RED2[:, 1:2],
                                     in_=YO[tt % 2][:, 512:1024].rearrange(
                                         "p (b k) -> p b k", k=512),
                                     axis=AX.X, op=ALU.add).then_inc(sDVE, 1)

        _apply_swaps(nc)
    return nc


def kernel(L0_attn_X, L0_attn_W_in, L0_attn_b_in, L0_attn_W_out, L0_attn_b_out,
           L0_ffn_weight, L0_ffn_bias):
    X = np.asarray(L0_attn_X, dtype=np.float32)
    W_in = np.asarray(L0_attn_W_in, dtype=np.float32)
    b_in = np.asarray(L0_attn_b_in, dtype=np.float32)
    W_out = np.asarray(L0_attn_W_out, dtype=np.float32)
    b_out = np.asarray(L0_attn_b_out, dtype=np.float32)
    fw = np.asarray(L0_ffn_weight, dtype=np.float32)
    fb = np.asarray(L0_ffn_bias, dtype=np.float32)

    winT = np.ascontiguousarray(
        W_in.reshape(H, 256, 8, 128).transpose(0, 3, 2, 1).reshape(H * 128,
                                                                   2048))
    b3 = b_in.reshape(H, 4, DH)
    binq = np.ascontiguousarray(b3[:, 0, :].T)
    bink = np.ascontiguousarray(b3[:, 1, :].T)
    binvg = np.ascontiguousarray(b_in.reshape(H, 256)[:, 128:].T)
    woutT = np.ascontiguousarray(
        W_out.reshape(D, 8, 128).transpose(2, 1, 0).reshape(128, 8 * D))
    bout_bc = np.ascontiguousarray(np.broadcast_to(b_out, (128, D)))
    fb_bc = np.ascontiguousarray(np.broadcast_to(fb, (128, D)))
    idn = np.eye(128, dtype=np.float32)

    nc = build_nc()
    in_maps = []
    for b in range(NCORES):
        in_maps.append({
            "x": np.ascontiguousarray(X[b]),
            "winT": winT, "binq": binq, "bink": bink, "binvg": binvg,
            "woutT": woutT, "bout": bout_bc, "fw": fw, "fb": fb_bc,
            "idn": idn,
        })
    res = run_bass_kernel_spmd(nc, in_maps, core_ids=list(range(NCORES)))
    out = np.stack([r["y"] for r in res.results], axis=0)
    return out


if __name__ == "__main__":
    rng = np.random.default_rng(0)
    args = dict(
        L0_attn_X=rng.normal(size=(NCORES, T, D)).astype(np.float32),
        L0_attn_W_in=(rng.normal(size=(4 * D, D)) / 32).astype(np.float32),
        L0_attn_b_in=(rng.normal(size=(4 * D,)) * 0.1).astype(np.float32),
        L0_attn_W_out=(rng.normal(size=(D, D)) / 32).astype(np.float32),
        L0_attn_b_out=(rng.normal(size=(D,)) * 0.1).astype(np.float32),
        L0_ffn_weight=(rng.normal(size=(D, D)) * 0.02).astype(np.float32),
        L0_ffn_bias=(rng.normal(size=(D,)) * 0.02).astype(np.float32),
    )
    y = kernel(**args)
    print(y.shape, y.dtype, float(np.abs(y).mean()))


# version 09
# speedup vs baseline: 1.4512x; 1.4512x over previous
import numpy as np
from contextlib import ExitStack

import concourse.bass as bass
import concourse.mybir as mybir
from concourse.bass_utils import run_bass_kernel_spmd

F32 = mybir.dt.float32
BF16 = mybir.dt.bfloat16
AF = mybir.ActivationFunctionType
ALU = mybir.AluOpType
AX = mybir.AxisListType

T = 4096
D = 1024
H = 16
DH = 64
NCORES = 8
NTT = T // 128
TQ = 1024
NQB = T // TQ
NKC = T // 128
EPS = 1e-4
EPS_LN = 1e-5
CLAMP = 12000.0
M1 = 1.5 * (2.0 ** 20)
DBG_Q2OFF = False

_swap_to = {}


def _rs(bi):
    _swap_to[bi.ins.name] = AF.Rsqrt
    return bi


def _apply_swaps(nc):
    for blk in nc.m.functions[0].blocks:
        insts = blk.instructions
        for i, inst in enumerate(insts):
            if inst.opcode == "Activation" and inst.name in _swap_to:
                clone = mybir.InstActivation(
                    name=inst.name, ins=list(inst.ins), outs=list(inst.outs),
                    func=_swap_to[inst.name])
                clone.merge_dependencies_from(inst)
                clone.sync_info = inst.sync_info
                clone.engine = inst.engine
                insts[i] = clone


ACT_P0 = 6 * NTT
ACT_PRO = 64
ACT_QB = NKC + 3
ACT_HEAD = ACT_PRO + NQB * ACT_QB
ACT_P2B = ACT_P0 + H * ACT_HEAD
ACT_TTB = ACT_P2B + 24

DVE_P0 = NTT
DVE_HEAD = NQB * (NKC + 2)
DVE_P2B = DVE_P0 + H * DVE_HEAD
DVE_TTB = DVE_P2B + 16

PE_P0 = 2 * NTT
PE_PRO = 56
PE_QB = 2 * NKC + 4
PE_HEAD = PE_PRO + NQB * PE_QB
PE_P2B = PE_P0 + H * PE_HEAD


def act_probase(h):
    return ACT_P0 + h * ACT_HEAD


def act_qbbase(h, qb):
    return act_probase(h) + ACT_PRO + qb * ACT_QB


def act_sig(h, qb, kc):
    return act_qbbase(h, qb) + kc + 1


def act_aevac(h, qb):
    return act_qbbase(h, qb) + NKC + 1


def act_ofevac(h, qb, half):
    return act_qbbase(h, qb) + NKC + 2 + half


def f_qev(nb):
    return 1 + nb


def f_kev(nb):
    return 9 + nb


def f_vgev(nb):
    return 17 + nb if nb < 4 else 21 + nb


def f_vgtrev(g):
    return 21 + g if g < 4 else 25 + g


def f_qsq(q):
    return 33 + 3 * q


def f_q2ev(nb):
    return 34 + 3 * (nb // 2) + (nb % 2)


def f_ksq(q):
    return 45 + 5 * q


def f_k2hi(nb):
    return 46 + 5 * (nb // 2) + 2 * (nb % 2)


def f_k2lo(nb):
    return 47 + 5 * (nb // 2) + 2 * (nb % 2)


def act_wdq_sq(i):
    return ACT_P2B + 3 * i + 1


def act_wdq_ev(i, half):
    return ACT_P2B + 3 * i + 2 + half


def act_ln2sq(tt):
    return ACT_TTB + 11 * tt + 1


def act_ln2sum(tt):
    return ACT_TTB + 11 * tt + 2


def act_x1(tt):
    return ACT_TTB + 11 * tt + 4


def act_xqsq(tt):
    return ACT_TTB + 11 * tt + 5


def act_xdf(tt, half):
    return ACT_TTB + 11 * tt + 6 + half


def act_ln3sq(tt):
    return ACT_TTB + 11 * tt + 8


def act_ln3sum(tt):
    return ACT_TTB + 11 * tt + 9


def act_y(tt):
    return ACT_TTB + 11 * tt + 11


def dve_headbase(h):
    return DVE_P0 + h * DVE_HEAD


def dve_z(h, qb, kc):
    return dve_headbase(h) + qb * (NKC + 2) + kc + 1


def dve_otm(h, qb, half):
    return dve_headbase(h) + qb * (NKC + 2) + NKC + half + 1


def dve_wdq_y(i):
    return DVE_P2B + 2 * i + 1


def dve_wdq_xd(i):
    return DVE_P2B + 2 * i + 2


def dve_r1(tt):
    return DVE_TTB + 6 * tt + 1


def dve_red2(tt):
    return DVE_TTB + 6 * tt + 2


def dve_x1q_y(tt):
    return DVE_TTB + 6 * tt + 3


def dve_x1q_xd(tt):
    return DVE_TTB + 6 * tt + 4


def dve_r2(tt):
    return DVE_TTB + 6 * tt + 5


def dve_red3(tt):
    return DVE_TTB + 6 * tt + 6


def tile_of(gi):
    return gi // (NQB * NKC), (gi // NKC) % NQB, gi % NKC


def build_nc(stage="full"):
    if stage.startswith("h") and stage[1:].isdigit():
        do_heads = int(stage[1:])
        do_attn = True
        do_p2 = False
    else:
        do_heads = {"p0": 0, "pro": 1, "attn1": 1, "noph2": H, "full": H}[stage]
        do_attn = stage in ("attn1", "noph2", "full")
        do_p2 = stage == "full"
    _swap_to.clear()
    nc = bass.Bass("TRN2", target_bir_lowering=False,
                   dynamic_dma_scratch_size=6144)

    x_h = nc.dram_tensor("x", [T, D], F32, kind="ExternalInput").ap()
    winT_h = nc.dram_tensor("winT", [H * 128, 8 * 256], F32,
                            kind="ExternalInput").ap()
    binq_h = nc.dram_tensor("binq", [64, H], F32, kind="ExternalInput").ap()
    bink_h = nc.dram_tensor("bink", [64, H], F32, kind="ExternalInput").ap()
    binvg_h = nc.dram_tensor("binvg", [128, H], F32, kind="ExternalInput").ap()
    woutT_h = nc.dram_tensor("woutT", [128, 8 * D], F32,
                             kind="ExternalInput").ap()
    bout_h = nc.dram_tensor("bout", [128, D], F32, kind="ExternalInput").ap()
    fw_h = nc.dram_tensor("fw", [D, D], F32, kind="ExternalInput").ap()
    fb_h = nc.dram_tensor("fb", [128, D], F32, kind="ExternalInput").ap()
    idn_h = nc.dram_tensor("idn", [128, 128], F32, kind="ExternalInput").ap()
    y_h = nc.dram_tensor("y", [T, D], F32, kind="ExternalOutput").ap()
    hN = stage.startswith("h") and stage[1:].isdigit()
    dbg = {}
    if stage != "full":
        dbg["xnf"] = nc.dram_tensor("dbg_xnf", [128, 8 * T], BF16,
                                    kind="ExternalOutput").ap()
        if stage in ("pro", "attn1", "noph2") or hN:
            dbg["qaug"] = nc.dram_tensor("dbg_qaug", [97, T], BF16,
                                         kind="ExternalOutput").ap()
            dbg["kaug"] = nc.dram_tensor("dbg_kaug", [97, T], BF16,
                                         kind="ExternalOutput").ap()
            dbg["vaug"] = nc.dram_tensor("dbg_vaug", [128, NKC * 65], BF16,
                                         kind="ExternalOutput").ap()
            dbg["gtm"] = nc.dram_tensor("dbg_gtm", [128, 2048], F32,
                                        kind="ExternalOutput").ap()
        if stage in ("attn1", "noph2") or hN:
            dbg["of"] = nc.dram_tensor("dbg_of", [128, 8 * T], BF16,
                                       kind="ExternalOutput").ap()
            dbg["vgf"] = nc.dram_tensor("dbg_vgf", [128, 2048], F32,
                                        kind="ExternalOutput").ap()
            dbg["otm"] = nc.dram_tensor("dbg_otm", [128, 1024], F32,
                                        kind="ExternalOutput").ap()
            for hh in range(do_heads):
                dbg[f"vaug{hh}"] = nc.dram_tensor(f"dbg_vaug{hh}",
                                                  [128, NKC * 65], BF16,
                                                  kind="ExternalOutput").ap()
            for nm, shp, dt in [("wt0", [128, TQ], BF16), ("wt1", [128, TQ], BF16),
                                ("atm", [128, TQ], F32), ("zt0", [128, TQ], F32),
                                ("zt1", [128, TQ], F32), ("fwq", [128, D], F32),
                                ("sig0", [128, TQ], F32), ("sig1", [128, TQ], F32)]:
                dbg[nm] = nc.dram_tensor("dbg_" + nm, shp, dt,
                                         kind="ExternalOutput").ap()

    ctx = ExitStack()
    sb = lambda name, shape, dt: ctx.enter_context(nc.sbuf_tensor(name, shape, dt))
    pst = lambda name, shape: ctx.enter_context(nc.psum_tensor(name, shape, F32))
    sem = lambda name: ctx.enter_context(nc.semaphore(name))

    with ctx:
        IDN = sb("IDN", [128, 128], F32)
        BQ = sb("BQ", [64, H], F32)
        BQm2 = sb("BQm2", [64, H], F32)
        BK = sb("BK", [64, H], F32)
        BVG = sb("BVG", [128, H], F32)
        BOB = sb("BOB", [128, D], F32)
        FB = sb("FB", [128, D], F32)
        ONESK = sb("ONESK", [64, 97], BF16)
        ONESQ = ONESK[:, 0:65]
        NEGL = sb("NEGL", [65, 97], BF16)
        STAT = sb("STAT", [128, 16], F32)
        RED2 = sb("RED2", [128, 2], F32)
        RCOL = sb("RCOL", [128, 8], F32)
        EPSC = sb("EPSC", [128, 1], F32)
        BM = sb("BM", [128, 32], F32)
        SCL = sb("SCL", [128, 32], F32)
        RSC = sb("RSC", [128, 32], F32)

        XT = [sb(f"XT{i}", [128, D], F32) for i in range(2)]
        XNF = sb("XNF", [128, 8 * T], BF16)
        OF = sb("OF", [128, 8 * T], BF16)
        WITB = sb("WITB", [128, 8 * 256], BF16)
        QAUG = sb("QAUG", [97, T], BF16)
        KAUG = sb("KAUG", [97, T], BF16)
        VGF = sb("VGF", [128, 2048], F32)
        VAUG = sb("VAUG", [128, NKC * 65], BF16)
        GTM = sb("GTM", [128, 2048], F32)

        SIG = [sb(f"SIG{i}", [128, TQ], F32) for i in range(2)]
        ATm = sb("ATm", [128, TQ], F32)
        ZT = [sb(f"ZT{i}", [128, TQ], F32) for i in range(2)]
        WTl = [sb(f"WTb{i}", [128, TQ], BF16) for i in range(2)]
        WTl.append(XT[0][:].bitcast(BF16))
        ZT.append(XT[1])

        def wtl_ap(i, a=0, b=TQ):
            return WTl[i][:, a:b] if i < 2 else WTl[2][:, a:b]
        OTM = sb("OTM", [128, 1024], F32)
        FWQ = sb("FWQ", [128, D], F32)
        X1T = [VGF[:, 0:D], VGF[:, D:2 * D]]
        YO = [GTM[:, 0:D], GTM[:, D:2 * D]]

        XNB = ZT[0]
        LNSa = XNB
        SQS = WTl[0][0:64, :]
        WOT = XNF[:, 0:8 * D]
        WDT = XNF[:, 8 * D:16 * D]
        XDF = XNF[:, 16 * D:17 * D]
        FWT = OTM
        AEV = FWQ[0:65, :]
        X2B = FWQ
        LNS2 = ATm
        QI = ATm[:].bitcast(mybir.dt.int32)
        QY = SIG[0]
        QC = SIG[1]
        QT1 = ZT[0]
        XD = ZT[1]

        PS_SQ = [pst("PS_SQ0", [128, TQ]), pst("PS_SQ1", [128, TQ])]
        PS_A = pst("PS_A", [97, TQ])
        PS_TR = [pst("PS_TR0", [128, 512]), pst("PS_TR1", [128, 512])]

        sIN = sem("sIN")
        sWL = sem("sWL")
        sOUT = sem("sOUT")
        sPE = sem("sPE")
        sACT = sem("sACT")
        sDVE = sem("sDVE")
        sQ2 = sem("sQ2")
        sQM = sem("sQM")

        in_marks = {}
        in_cnt = [0]

        def in_mark(key):
            in_cnt[0] += 16
            in_marks[key] = in_cnt[0]

        wl_marks = {}
        wl_cnt = [0]

        def wl_mark(key):
            wl_cnt[0] += 16
            wl_marks[key] = wl_cnt[0]

        pe_marks = {}
        pe_cnt = [0]

        def pe_mark(key):
            pe_cnt[0] += 1
            pe_marks[key] = pe_cnt[0]

        block = ctx.enter_context(nc.Block())

        @block.sync
        def _(sync):
            for ap_sb, ap_h in [(IDN[:], idn_h), (BQ[:], binq_h), (BK[:], bink_h),
                                (BVG[:], binvg_h), (BOB[:], bout_h), (FB[:], fb_h)]:
                sync.dma_start(out=ap_sb, in_=ap_h).then_inc(sIN, 16)
                in_cnt[0] += 16
            in_marks["consts"] = in_cnt[0]
            for i in range(NTT):
                if i >= 2:
                    sync.wait_ge(sACT, 6 * (i - 2) + 4)
                    sync.wait_ge(sDVE, i - 1)
                sync.dma_start(out=XT[i % 2][:],
                               in_=x_h[i * 128:(i + 1) * 128, :]).then_inc(sIN, 16)
                in_mark(("x0", i))
            for h in range(do_heads):
                sync.wait_ge(sACT, act_probase(h) + f_q2ev(7))
                sync.dma_start(out=QAUG[66:67, :] if DBG_Q2OFF else QAUG[65:66, :],
                               in_=QAUG[64:65, :]).then_inc(sQ2, 16)
                if f"vaug{h}" in dbg:
                    sync.wait_ge(sACT, act_probase(h) + ACT_PRO)
                    sync.dma_start(out=dbg[f"vaug{h}"], in_=VAUG[:]
                                   ).then_inc(sOUT, 16)
            if do_p2:
                sync.wait_ge(sPE, PE_P2B)
                sync.wait_ge(sWL, 16 * (H + 1))
                for i in range(8):
                    if i >= 1:
                        sync.wait_ge(sDVE, dve_wdq_y(i - 1))
                    sync.dma_start(out=FWT[:],
                                   in_=fw_h[128 * i:128 * (i + 1), :]
                                   ).then_inc(sWL, 16)
                    wl_marks[("fw", i)] = 16 * (H + 1) + 16 * (i + 1)
                sync.wait_ge(sACT, ACT_P0)
                for tt in range(NTT):
                    if tt >= 2:
                        sync.wait_ge(sDVE, dve_r1(tt - 2))
                    sync.dma_start(out=XT[tt % 2][:],
                                   in_=x_h[tt * 128:(tt + 1) * 128, :]
                                   ).then_inc(sIN, 16)
                    in_mark(("x2", tt))
                    if tt >= 1:
                        sync.wait_ge(sACT, act_y(tt - 1))
                        sync.dma_start(out=y_h[(tt - 1) * 128:tt * 128, :],
                                       in_=YO[(tt - 1) % 2][:]).then_inc(sOUT, 16)
                sync.wait_ge(sACT, act_y(NTT - 1))
                sync.dma_start(out=y_h[(NTT - 1) * 128:NTT * 128, :],
                               in_=YO[(NTT - 1) % 2][:]).then_inc(sOUT, 16)
                sync.wait_ge(sOUT, 16 * NTT)
            if stage != "full":
                if stage == "p0":
                    sync.wait_ge(sACT, ACT_P0)
                elif stage == "pro":
                    sync.wait_ge(sACT, act_probase(0) + ACT_PRO)
                else:
                    sync.wait_ge(sACT, act_probase(do_heads - 1) + ACT_HEAD)
                nst = 16 * sum(1 for k in dbg if k.startswith("vaug") and k != "vaug")
                for key, hbm in dbg.items():
                    if key.startswith("vaug") and key != "vaug":
                        continue
                    src = {"xnf": XNF[:], "qaug": QAUG[:], "kaug": KAUG[:],
                           "vaug": VAUG[:], "gtm": GTM[:], "of": OF[:],
                           "vgf": VGF[:], "otm": OTM[:], "wt0": WTl[0][:],
                           "wt1": WTl[1][:], "atm": ATm[:], "zt0": ZT[0][:],
                           "zt1": ZT[1][:], "fwq": FWQ[:], "sig0": SIG[0][:],
                           "sig1": SIG[1][:]}[key]
                    sync.dma_start(out=hbm, in_=src).then_inc(sOUT, 16)
                    nst += 16
                sync.dma_start(out=y_h[0:128, :], in_=XT[0][:]).then_inc(sOUT, 16)
                nst += 16
                sync.wait_ge(sOUT, nst)

        @block.gpsimd
        def _(gpsimd):
            for h in range(do_heads):
                if h >= 1:
                    gpsimd.wait_ge(sPE, PE_P0 + (h - 1) * PE_HEAD + 28)
                gpsimd.dma_start(out=WITB[:],
                                 in_=winT_h[128 * h:128 * (h + 1), :]
                                 ).then_inc(sWL, 16)
                wl_mark(("wit", h))
            if not do_p2:
                return
            gpsimd.wait_ge(sPE, PE_P0 + (H - 1) * PE_HEAD + 28)
            gpsimd.dma_start(out=WOT, in_=woutT_h).then_inc(sWL, 16)
            wl_mark("wot")

        def emit_wv(tensor, lh, lqb, lkc, ab):
            gi = (lh * NQB + lqb) * NKC + lkc
            tensor.wait_ge(sDVE, dve_z(lh, lqb, lkc))
            if lqb == 0:
                tensor.wait_ge(sACT, ab + f_vgtrev(lkc // 4))
            if lkc == 0 and (lh * NQB + lqb) > 0:
                pq = lh * NQB + lqb - 1
                tensor.wait_ge(sACT, act_aevac(pq // NQB, pq % NQB))
            for half in range(2):
                last = tensor.matmul(PS_A[0:65, half * 512:(half + 1) * 512],
                                     VAUG[:, lkc * 65:(lkc + 1) * 65],
                                     wtl_ap(gi % 3, half * 512,
                                            (half + 1) * 512),
                                     start=(lkc == 0), stop=(lkc == NKC - 1))
            last.then_inc(sPE, 1)
            pe_mark(("wv", lh, lqb, lkc))

        @block.tensor
        def _(tensor):
            tensor.wait_ge(sIN, in_marks["consts"])
            for i in range(NTT):
                tensor.wait_ge(sACT, 6 * i + 4)
                for half in range(2):
                    if i > 0:
                        tensor.wait_ge(sACT, 6 * (i - 1) + 5 + half)
                    bank = PS_TR[half]
                    for j in range(4):
                        c = half * 4 + j
                        last = tensor.transpose(bank[:, j * 128:(j + 1) * 128],
                                                XNB[:, c * 128:(c + 1) * 128],
                                                IDN[:])
                    last.then_inc(sPE, 1)
                    pe_mark(("xntr", i, half))

            for h in range(do_heads):
                ab = act_probase(h)
                tensor.wait_ge(sWL, wl_marks[("wit", h)])
                if h == 0:
                    tensor.wait_ge(sACT, ACT_P0)
                for nb in range(8):
                    bank = PS_SQ[nb % 2]
                    if h > 0 and nb < 2:
                        tensor.wait_ge(sACT, act_sig(h - 1, NQB - 1, NKC - 1))
                    if nb >= 2:
                        tensor.wait_ge(sACT, ab + f_qev(nb - 2))
                    for c in range(8):
                        last = tensor.matmul(
                            bank[0:64, 0:512],
                            WITB[:, c * 256:c * 256 + 64],
                            XNF[:, c * T + nb * 512:c * T + (nb + 1) * 512],
                            start=(c == 0), stop=(c == 7))
                    last.then_inc(sPE, 1)
                    pe_mark(("qproj", h, nb))
                for nb in range(8):
                    bank = PS_SQ[nb % 2]
                    tensor.wait_ge(sACT,
                                   ab + (f_qev(6 + nb) if nb < 2 else f_kev(nb - 2)))
                    for c in range(8):
                        last = tensor.matmul(
                            bank[0:64, 0:512],
                            WITB[:, c * 256 + 64:c * 256 + 128],
                            XNF[:, c * T + nb * 512:c * T + (nb + 1) * 512],
                            start=(c == 0), stop=(c == 7))
                    last.then_inc(sPE, 1)
                    pe_mark(("kproj", h, nb))
                for nb in range(4):
                    bank = PS_SQ[nb % 2]
                    tensor.wait_ge(sACT,
                                   ab + (f_kev(6 + nb) if nb < 2 else f_vgev(nb - 2)))
                    for c in range(8):
                        last = tensor.matmul(
                            bank[:, 0:512],
                            WITB[:, c * 256 + 128:c * 256 + 256],
                            XNF[:, c * T + nb * 512:c * T + (nb + 1) * 512],
                            start=(c == 0), stop=(c == 7))
                    last.then_inc(sPE, 1)
                    pe_mark(("vgproj", h, nb))
                for g in range(4):
                    bank = PS_TR[g % 2]
                    tensor.wait_ge(sACT, ab + f_vgev(g))
                    if g >= 2:
                        tensor.wait_ge(sACT, ab + f_vgtrev(g - 2))
                    elif h > 0:
                        tensor.wait_ge(sACT, act_ofevac(h - 1, NQB - 1, g))
                    for j in range(4):
                        kc = g * 4 + j
                        last = tensor.transpose(
                            bank[:, j * 128:(j + 1) * 128],
                            VGF[:, (kc % 16) * 128:(kc % 16 + 1) * 128],
                            IDN[:])
                    last.then_inc(sPE, 1)
                    pe_mark(("vgtr", h, g))
                for nb in range(4, 8):
                    bank = PS_SQ[nb % 2]
                    tensor.wait_ge(sACT, ab + f_vgev(nb - 2))
                    for c in range(8):
                        last = tensor.matmul(
                            bank[:, 0:512],
                            WITB[:, c * 256 + 128:c * 256 + 256],
                            XNF[:, c * T + nb * 512:c * T + (nb + 1) * 512],
                            start=(c == 0), stop=(c == 7))
                    last.then_inc(sPE, 1)
                    pe_mark(("vgproj", h, nb))
                for g in range(4, 8):
                    bank = PS_TR[g % 2]
                    tensor.wait_ge(sACT, ab + f_vgev(g))
                    tensor.wait_ge(sACT, ab + f_vgtrev(g - 2))
                    for j in range(4):
                        kc = g * 4 + j
                        last = tensor.transpose(
                            bank[:, j * 128:(j + 1) * 128],
                            VGF[:, (kc % 16) * 128:(kc % 16 + 1) * 128],
                            IDN[:])
                    last.then_inc(sPE, 1)
                    pe_mark(("vgtr", h, g))
                for nb in range(8):
                    tensor.wait_ge(sACT, ab + f_qsq(nb // 2))
                    if nb == 0 and h > 0:
                        tensor.wait_ge(sACT, act_aevac(h - 1, NQB - 1))
                    if nb >= 1:
                        tensor.wait_ge(sACT, ab + f_q2ev(nb - 1))
                    tensor.matmul(PS_A[0:65, 0:512], ONESQ,
                                  SQS[:, (nb % 2) * 512:(nb % 2 + 1) * 512],
                                  start=True, stop=True).then_inc(sPE, 1)
                    pe_mark(("q2", h, nb))
                for nb in range(8):
                    tensor.wait_ge(sACT, ab + f_ksq(nb // 2))
                    if nb == 0:
                        tensor.wait_ge(sACT, ab + f_q2ev(7))
                    if nb >= 1:
                        tensor.wait_ge(sACT, ab + f_k2lo(nb - 1))
                    tensor.matmul(PS_A[0:97, 0:512], ONESK[:],
                                  SQS[:, (nb % 2) * 512:(nb % 2 + 1) * 512],
                                  start=True, stop=False).then_inc(sPE, 1)
                    pe_mark(("k2", h, nb))
                    tensor.wait_ge(sACT, ab + f_k2hi(nb))
                    tensor.matmul(PS_A[0:97, 0:512], NEGL[64:65, :],
                                  KAUG[64:65, nb * 512:(nb + 1) * 512],
                                  start=False, stop=True).then_inc(sPE, 1)
                    pe_mark(("k2b", h, nb))

                for qb in range(NQB if do_attn else 0):
                    for kc in range(NKC):
                        gi = (h * NQB + qb) * NKC + kc
                        buf = gi % 2
                        bank = PS_SQ[buf]
                        if qb == 0 and kc < 2:
                            tensor.wait_ge(sACT, ab + f_vgev(6 + kc))
                        else:
                            ph, pqb, pkc = tile_of(gi - 2)
                            tensor.wait_ge(sACT, act_sig(ph, pqb, pkc))
                        if qb == 0 and kc == 0:
                            tensor.wait_ge(sACT, ab + f_k2lo(7))
                            tensor.wait_ge(sQM, h + 1)
                        for half in range(2):
                            last = tensor.matmul(
                                bank[:, half * 512:(half + 1) * 512],
                                KAUG[:, kc * 128:(kc + 1) * 128],
                                QAUG[:, qb * TQ + half * 512:qb * TQ + half * 512 + 512],
                                start=True, stop=True)
                        last.then_inc(sPE, 1)
                        pe_mark(("sq", h, qb, kc))
                        if kc >= 1:
                            emit_wv(tensor, h, qb, kc - 1, ab)
                    emit_wv(tensor, h, qb, NKC - 1, ab)
                    tensor.wait_ge(sACT, act_aevac(h, qb))
                    for half in range(2):
                        bank = PS_TR[half]
                        if qb == 0:
                            tensor.wait_ge(sACT, ab + f_vgtrev(6 + half))
                        else:
                            tensor.wait_ge(sACT, act_ofevac(h, qb - 1, half))
                        for j in range(4):
                            g = half * 4 + j
                            last = tensor.transpose(
                                bank[:, j * 130:j * 130 + 65],
                                AEV[:, g * 128:(g + 1) * 128],
                                IDN[0:65, 0:65])
                        last.then_inc(sPE, 1)
                        pe_mark(("atr", h, qb, half))
                    for half in range(2):
                        bank = PS_TR[half]
                        tensor.wait_ge(sDVE, dve_otm(h, qb, half))
                        for j in range(4):
                            g = half * 4 + j
                            last = tensor.transpose(
                                bank[:, j * 128:(j + 1) * 128],
                                OTM[:, g * 128:(g + 1) * 128], IDN[:])
                        last.then_inc(sPE, 1)
                        pe_mark(("otr", h, qb, half))

            if not do_p2:
                return
            for i in range(8):
                for half in range(2):
                    bank = PS_TR[half]
                    tensor.wait_ge(sDVE, dve_wdq_xd(i))
                    if i == 0:
                        tensor.wait_ge(sACT, act_ofevac(H - 1, NQB - 1, half))
                    else:
                        tensor.wait_ge(sACT, act_wdq_ev(i - 1, half))
                    for j in range(4):
                        c = half * 4 + j
                        last = tensor.transpose(
                            bank[:, j * 128:(j + 1) * 128],
                            FWQ[:, c * 128:(c + 1) * 128], IDN[:])
                    last.then_inc(sPE, 1)
                    pe_mark(("wdtr", i, half))
            tensor.wait_ge(sWL, wl_marks["wot"])
            tensor.wait_ge(sACT, ACT_P2B)
            for tt in range(NTT):
                if tt >= 1:
                    tensor.wait_ge(sDVE, dve_r1(tt - 1))
                for c in range(8):
                    for half in range(2):
                        last = tensor.matmul(
                            PS_SQ[0][:, half * 512:(half + 1) * 512],
                            OF[:, c * T + tt * 128:c * T + (tt + 1) * 128],
                            WOT[:, c * D + half * 512:c * D + (half + 1) * 512],
                            start=(c == 0), stop=(c == 7))
                last.then_inc(sPE, 1)
                pe_mark(("z", tt))
                for half in range(2):
                    bank = PS_TR[half]
                    tensor.wait_ge(sDVE, dve_x1q_xd(tt))
                    if tt == 0:
                        tensor.wait_ge(sACT, act_wdq_ev(7, half))
                    else:
                        tensor.wait_ge(sACT, act_xdf(tt - 1, half))
                    for j in range(4):
                        c = half * 4 + j
                        last = tensor.transpose(
                            bank[:, j * 128:(j + 1) * 128],
                            XD[:, c * 128:(c + 1) * 128], IDN[:])
                    last.then_inc(sPE, 1)
                    pe_mark(("xdtr", tt, half))
                tensor.wait_ge(sACT, act_xdf(tt, 1))
                if tt >= 1:
                    tensor.wait_ge(sDVE, dve_r2(tt - 1))
                for c in range(8):
                    for half in range(2):
                        last = tensor.matmul(
                            PS_SQ[1][:, half * 512:(half + 1) * 512],
                            XDF[:, c * 128:(c + 1) * 128],
                            WDT[:, c * D + half * 512:c * D + (half + 1) * 512],
                            start=(c == 0), stop=(c == 7))
                last.then_inc(sPE, 1)
                pe_mark(("ffn", tt))

        def ln_stats(scalar, red2):
            scalar.activation(STAT[:, 0:1], red2[:, 0:1], AF.Prelu, alpha=1.0,
                              bias=red2[:, 1:2], scale=1.0).then_inc(sACT, 1)
            scalar.activation(STAT[:, 7:8], STAT[:, 1:2], AF.Prelu, alpha=1.0,
                              scale=1.0 / D)
            scalar.activation(STAT[:, 6:7], STAT[:, 0:1], AF.Prelu, alpha=1.0,
                              scale=-1.0 / D)
            scalar.activation(STAT[:, 0:1], STAT[:, 0:1], AF.Prelu, alpha=1.0,
                              scale=1.0 / D)
            scalar.activation(STAT[:, 5:6], STAT[:, 0:1], AF.Prelu, alpha=1.0,
                              scale=STAT[:, 6:7])
            scalar.activation(STAT[:, 2:3], STAT[:, 5:6], AF.Prelu, alpha=1.0,
                              bias=STAT[:, 7:8], scale=1.0)
            _rs(scalar.activation(STAT[:, 3:4], STAT[:, 2:3], AF.Square,
                                  bias=EPSC[:, 0:1], scale=1.0))
            scalar.activation(STAT[:, 4:5], STAT[:, 3:4], AF.Prelu, alpha=1.0,
                              scale=STAT[:, 6:7])
            scalar.activation(STAT[:, 8:9], EPSC[:, 0:1], AF.Prelu, alpha=1.0,
                              scale=1.0).then_inc(sACT, 1)

        @block.scalar
        def _(scalar):
            for i in range(NTT):
                scalar.wait_ge(sIN, in_marks[("x0", i)])
                if i >= 1:
                    scalar.wait_ge(sPE, pe_marks[("xntr", i - 1, 1)])
                xt = XT[i % 2]
                scalar.activation(LNSa[:], xt[:], AF.Square,
                                  accum_out=STAT[:, 1:2]).then_inc(sACT, 1)
                scalar.wait_ge(sDVE, i + 1)
                ln_stats(scalar, RED2)
                scalar.activation(XNB[:], xt[:], AF.Prelu, alpha=1.0,
                                  bias=STAT[:, 4:5], scale=STAT[:, 3:4]
                                  ).then_inc(sACT, 1)
                for half in range(2):
                    scalar.wait_ge(sPE, pe_marks[("xntr", i, half)])
                    src = PS_TR[half][:].rearrange("p (j f) -> p j f", f=128)
                    dst = XNF[:].rearrange("p (c f) -> p c f", f=T)[
                        :, half * 4:(half + 1) * 4, i * 128:(i + 1) * 128]
                    scalar.activation(dst, src, AF.Copy).then_inc(sACT, 1)

            for h in range(do_heads):
                ab = act_probase(h)
                if h == 0:
                    scalar.wait_ge(sDVE, DVE_P0)
                for nb in range(8):
                    scalar.wait_ge(sPE, pe_marks[("qproj", h, nb)])
                    bank = PS_SQ[nb % 2]
                    scalar.activation(QAUG[0:64, nb * 512:(nb + 1) * 512],
                                      bank[0:64, 0:512], AF.Prelu, alpha=1.0,
                                      bias=BQm2[:, h:h + 1], scale=-2.0
                                      ).then_inc(sACT, 1)
                for nb in range(8):
                    scalar.wait_ge(sPE, pe_marks[("kproj", h, nb)])
                    bank = PS_SQ[nb % 2]
                    scalar.activation(KAUG[0:64, nb * 512:(nb + 1) * 512],
                                      bank[0:64, 0:512], AF.Prelu, alpha=1.0,
                                      bias=BK[:, h:h + 1], scale=1.0
                                      ).then_inc(sACT, 1)
                for nb in range(8):
                    scalar.wait_ge(sPE, pe_marks[("vgproj", h, nb)])
                    if nb >= 4:
                        scalar.wait_ge(sPE, pe_marks[("vgtr", h, nb - 4)])
                    bank = PS_SQ[nb % 2]
                    scalar.activation(VGF[:, (nb % 4) * 512:(nb % 4 + 1) * 512],
                                      bank[:, 0:512], AF.Prelu, alpha=1.0,
                                      bias=BVG[:, h:h + 1], scale=1.0
                                      ).then_inc(sACT, 1)
                    if nb == 3:
                        for g in range(4):
                            scalar.wait_ge(sPE, pe_marks[("vgtr", h, g)])
                            if g < 2 and h > 0:
                                scalar.wait_ge(
                                    sPE, pe_marks[("wv", h - 1, NQB - 1, NKC - 1)])
                                scalar.wait_ge(sDVE, dve_otm(h - 1, NQB - 1, 1))
                            bank2 = PS_TR[g % 2]
                            src = bank2[:].rearrange("p (j f) -> p j f", f=128)
                            vdst = VAUG[:].rearrange("p (k f) -> p k f", f=65)[
                                :, g * 4:(g + 1) * 4, 0:64]
                            scalar.activation(vdst, src[:, :, 0:64], AF.Copy)
                            gdst = GTM[:].rearrange("p (k f) -> p k f", f=DH)[
                                :, g * 4:(g + 1) * 4, :]
                            scalar.activation(gdst, src[:, :, 64:128], AF.Copy
                                              ).then_inc(sACT, 1)
                for g in range(4, 8):
                    scalar.wait_ge(sPE, pe_marks[("vgtr", h, g)])
                    bank2 = PS_TR[g % 2]
                    src = bank2[:].rearrange("p (j f) -> p j f", f=128)
                    vdst = VAUG[:].rearrange("p (k f) -> p k f", f=65)[
                        :, g * 4:(g + 1) * 4, 0:64]
                    scalar.activation(vdst, src[:, :, 0:64], AF.Copy)
                    gdst = GTM[:].rearrange("p (k f) -> p k f", f=DH)[
                        :, g * 4:(g + 1) * 4, :]
                    scalar.activation(gdst, src[:, :, 64:128], AF.Copy
                                      ).then_inc(sACT, 1)
                for q in range(4):
                    if q == 0:
                        if h > 0:
                            lg = h * 128 - 1
                            while lg % 3:
                                lg -= 1
                            scalar.wait_ge(sPE, pe_marks[("wv", *tile_of(lg))])
                        else:
                            scalar.wait_ge(sPE, pe_marks[("vgtr", h, 7)])
                    else:
                        scalar.wait_ge(sPE, pe_marks[("q2", h, 2 * q - 1)])
                    scalar.activation(SQS[:], QAUG[0:64, q * 1024:(q + 1) * 1024],
                                      AF.Square, scale=0.5).then_inc(sACT, 1)
                    for half in range(2):
                        nb = 2 * q + half
                        scalar.wait_ge(sPE, pe_marks[("q2", h, nb)])
                        if q == 0 and half == 0 and h > 0:
                            scalar.wait_ge(sQ2, 16 * h)
                        scalar.activation(QAUG[64:65, nb * 512:(nb + 1) * 512],
                                          PS_A[64:65, 0:512], AF.Copy
                                          ).then_inc(sACT, 1)
                for q in range(4):
                    scalar.wait_ge(sPE, pe_marks[("q2", h, 7)] if q == 0
                                   else pe_marks[("k2b", h, 2 * q - 1)])
                    scalar.activation(SQS[:], KAUG[0:64, q * 1024:(q + 1) * 1024],
                                      AF.Square, scale=1.0).then_inc(sACT, 1)
                    for half in range(2):
                        nb = 2 * q + half
                        scalar.wait_ge(sPE, pe_marks[("k2", h, nb)])
                        scalar.activation(KAUG[64:65, nb * 512:(nb + 1) * 512],
                                          PS_A[64:65, 0:512], AF.Copy
                                          ).then_inc(sACT, 1)
                        scalar.wait_ge(sPE, pe_marks[("k2b", h, nb)])
                        scalar.activation(KAUG[96:97, nb * 512:(nb + 1) * 512],
                                          PS_A[96:97, 0:512], AF.Copy
                                          ).then_inc(sACT, 1)

                for qb in range(NQB if do_attn else 0):
                    for kc in range(NKC):
                        gi = (h * NQB + qb) * NKC + kc
                        scalar.wait_ge(sPE, pe_marks[("sq", h, qb, kc)])
                        if gi >= 2:
                            ph, pqb, pkc = tile_of(gi - 2)
                            scalar.wait_ge(sDVE, dve_z(ph, pqb, pkc))
                        _rs(scalar.activation(SIG[gi % 2][:], PS_SQ[gi % 2][:],
                                              AF.Square)).then_inc(sACT, 1)
                    scalar.wait_ge(sPE, pe_marks[("wv", h, qb, NKC - 1)])
                    if h * NQB + qb > 0:
                        pq = h * NQB + qb - 1
                        scalar.wait_ge(sPE, pe_marks[("atr", pq // NQB, pq % NQB, 1)])
                    scalar.activation(AEV[:], PS_A[0:65, :], AF.Copy).then_inc(sACT, 1)
                    for half in range(2):
                        scalar.wait_ge(sPE, pe_marks[("otr", h, qb, half)])
                        bank = PS_TR[half]
                        part = (h % 2) * 64
                        base = (h // 2) * T + qb * TQ + half * 512
                        src = bank[part:part + 64, :].rearrange(
                            "p (g f) -> p g f", f=128)
                        dst = OF[part:part + 64, base:base + 512].rearrange(
                            "p (g f) -> p g f", f=128)
                        scalar.activation(dst, src, AF.Copy).then_inc(sACT, 1)

            if not do_p2:
                return
            for i in range(8):
                scalar.wait_ge(sDVE, dve_wdq_y(i))
                scalar.activation(LNS2[:], QY[:], AF.Square).then_inc(sACT, 1)
                for half in range(2):
                    scalar.wait_ge(sPE, pe_marks[("wdtr", i, half)])
                    bank = PS_TR[half]
                    src = bank[:].rearrange("p (j f) -> p j f", f=128)
                    dst = WDT.rearrange("p (c f) -> p c f", f=D)[
                        :, half * 4:(half + 1) * 4, i * 128:(i + 1) * 128]
                    scalar.activation(dst, src, AF.Copy).then_inc(sACT, 1)
            for tt in range(NTT):
                R = YO[tt % 2]
                scalar.wait_ge(sDVE, dve_r1(tt))
                scalar.activation(LNS2[:], R[:], AF.Square,
                                  accum_out=STAT[:, 1:2]).then_inc(sACT, 1)
                scalar.wait_ge(sDVE, dve_red2(tt))
                ln_stats(scalar, RED2)
                scalar.activation(X1T[tt % 2][:], R[:], AF.Prelu, alpha=1.0,
                                  bias=STAT[:, 4:5], scale=STAT[:, 3:4]
                                  ).then_inc(sACT, 1)
                scalar.wait_ge(sDVE, dve_x1q_y(tt))
                scalar.activation(LNS2[:], QY[:], AF.Square).then_inc(sACT, 1)
                for half in range(2):
                    scalar.wait_ge(sPE, pe_marks[("xdtr", tt, half)])
                    bank = PS_TR[half]
                    src = bank[:].rearrange("p (j f) -> p j f", f=128)
                    dst = XDF.rearrange("p (c f) -> p c f", f=128)[
                        :, half * 4:(half + 1) * 4, :]
                    scalar.activation(dst, src, AF.Copy).then_inc(sACT, 1)
                scalar.wait_ge(sDVE, dve_r2(tt))
                scalar.activation(LNS2[:], R[:], AF.Square,
                                  accum_out=STAT[:, 1:2]).then_inc(sACT, 1)
                scalar.wait_ge(sDVE, dve_red3(tt))
                ln_stats(scalar, RED2)
                scalar.activation(R[:], R[:], AF.Prelu, alpha=1.0, bias=STAT[:, 4:5],
                                  scale=STAT[:, 3:4]).then_inc(sACT, 1)

        @block.vector
        def _(vector):
            vector.wait_ge(sIN, in_marks["consts"])
            vector.tensor_scalar(out=BQm2[:], in0=BQ[:], scalar1=-2.0,
                                 scalar2=None, op0=ALU.mult)
            vector.memset(KAUG[64:96, :], 0.0)
            vector.memset(KAUG[64:66, :], 1.0)
            vector.memset(KAUG[96:97, :], 0.0)
            vector.memset(QAUG[64:96, :], 1.0)
            vector.memset(QAUG[96:97, :], 1.0)
            vector.memset(VAUG[:].rearrange("p (k f) -> p k f", f=65)[:, :, 64:65],
                          1.0)
            vector.memset(EPSC[:], EPS_LN)
            vector.memset(ONESK[:], 0.0)
            vector.memset(ONESK[:, 64:65], 1.0)
            vector.memset(ONESK[:, 96:97], 1.0)
            vector.memset(NEGL[64:65, :], 0.0)
            vector.memset(NEGL[64:65, 96:97], -1.0)

            for i in range(NTT):
                vector.wait_ge(sIN, in_marks[("x0", i)])
                if i >= 1:
                    vector.wait_ge(sACT, 6 * (i - 1) + 2)
                xt = XT[i % 2]
                vector.tensor_reduce(out=RED2[:],
                                     in_=xt[:].rearrange("p (b k) -> p b k",
                                                         k=512),
                                     axis=AX.X, op=ALU.add).then_inc(sDVE, 1)

            for h in range(do_heads):
                ab = act_probase(h)
                if do_attn:
                    vector.wait_ge(sQ2, 16 * (h + 1))
                    vector.memset(QAUG[64:65, :], 1.0).then_inc(sQM, 1)
                for qb in range(NQB if do_attn else 0):
                    for kc in range(NKC):
                        gi = (h * NQB + qb) * NKC + kc
                        buf = gi % 2
                        vector.wait_ge(sACT, act_sig(h, qb, kc))
                        for hf in range(2):
                            sl = slice(hf * 512, (hf + 1) * 512)
                            vector.tensor_scalar(out=ATm[:, sl],
                                                 in0=SIG[gi % 2][:, sl],
                                                 scalar1=CLAMP,
                                                 scalar2=-EPS * 100.0,
                                                 op0=ALU.min, op1=ALU.mult)
                        for hf in range(2):
                            sl = slice(hf * 512, (hf + 1) * 512)
                            vector.scalar_tensor_tensor(
                                out=ZT[gi % 3][:, sl], in0=ATm[:, sl],
                                scalar=100.0, in1=ATm[:, sl],
                                op0=ALU.add, op1=ALU.mult)
                        if gi >= 3:
                            ph, pqb, pkc = tile_of(gi - 3)
                            vector.wait_ge(sPE, pe_marks[("wv", ph, pqb, pkc)])
                        last = None
                        for hf in range(2):
                            sl = slice(hf * 512, (hf + 1) * 512)
                            last = vector.tensor_tensor(
                                out=wtl_ap(gi % 3, hf * 512, (hf + 1) * 512),
                                in0=ZT[gi % 3][:, sl], in1=ZT[gi % 3][:, sl],
                                op=ALU.mult)
                        last.then_inc(sDVE, 1)
                    for half in range(2):
                        vector.wait_ge(sPE, pe_marks[("atr", h, qb, half)])
                        bank = PS_TR[half]
                        for j in range(4):
                            vector.tensor_scalar(
                                out=RCOL[:, j:j + 1],
                                in0=bank[:, j * 130 + 64:j * 130 + 65],
                                scalar1=EPS, scalar2=None, op0=ALU.add)
                        for j in range(4):
                            vector.reciprocal(out=RCOL[:, 4 + j:5 + j],
                                              in_=RCOL[:, j:j + 1])
                        last = None
                        for j in range(4):
                            g = half * 4 + j
                            at = bank[:, j * 130:j * 130 + 65]
                            kcg = qb * 8 + g
                            col = g * 128 + (h % 2) * 64
                            last = vector.scalar_tensor_tensor(
                                out=OTM[:, col:col + 64],
                                in0=at[:, 0:64], scalar=RCOL[:, 4 + j:5 + j],
                                in1=GTM[:, kcg * DH:(kcg + 1) * DH],
                                op0=ALU.mult, op1=ALU.mult)
                        last.then_inc(sDVE, 1)

            def quant(xsrc, xdst, gate_sq, gate_prev_sq, xd_guard):
                hs = [slice(0, 512), slice(512, 1024)]
                bs = [slice(0, 16), slice(16, 32)]
                x3h = [xsrc[:, s].rearrange("p (b k) -> p b k", k=32) for s in hs]
                for hf in range(2):
                    vector.tensor_reduce(out=BM[:, bs[hf]], in_=x3h[hf],
                                         axis=AX.X, op=ALU.max,
                                         apply_absolute_value=True)
                for hf in range(2):
                    vector.tensor_scalar(out=SCL[:, bs[hf]], in0=BM[:, bs[hf]],
                                         scalar1=1.0 / 7.5, scalar2=1e-30,
                                         op0=ALU.mult, op1=ALU.max)
                for hf in range(2):
                    vector.reciprocal(out=RSC[:, bs[hf]], in_=SCL[:, bs[hf]])
                if gate_prev_sq is not None:
                    vector.wait_ge(sACT, gate_prev_sq)
                last = None
                for hf in range(2):
                    last = vector.tensor_tensor(
                        out=QY[:, hs[hf]].rearrange("p (b k) -> p b k", k=32),
                        in0=x3h[hf],
                        in1=RSC[:, bs[hf]].unsqueeze(2).broadcast_to(
                            [128, 16, 32]),
                        op=ALU.mult)
                last.then_inc(sDVE, 1)
                vector.wait_ge(sACT, gate_sq)
                for hf in range(2):
                    vector.tensor_scalar(out=QT1[:, hs[hf]], in0=LNS2[:, hs[hf]],
                                         scalar1=4.0, scalar2=None,
                                         op0=ALU.is_ge)
                for hf in range(2):
                    vector.tensor_scalar(out=QC[:, hs[hf]], in0=LNS2[:, hs[hf]],
                                         scalar1=16.0, scalar2=None,
                                         op0=ALU.is_ge)
                for hf in range(2):
                    vector.tensor_scalar(out=QT1[:, hs[hf]], in0=QT1[:, hs[hf]],
                                         scalar1=-0.5, scalar2=1.0,
                                         op0=ALU.mult, op1=ALU.add)
                for hf in range(2):
                    vector.tensor_scalar(out=QC[:, hs[hf]], in0=QC[:, hs[hf]],
                                         scalar1=-0.5, scalar2=1.0,
                                         op0=ALU.mult, op1=ALU.add)
                for hf in range(2):
                    vector.scalar_tensor_tensor(out=QC[:, hs[hf]],
                                                in0=QC[:, hs[hf]], scalar=8.0,
                                                in1=QT1[:, hs[hf]],
                                                op0=ALU.mult, op1=ALU.mult)
                for hf in range(2):
                    vector.tensor_tensor(out=QT1[:, hs[hf]], in0=QY[:, hs[hf]],
                                         in1=QC[:, hs[hf]], op=ALU.mult)
                for hf in range(2):
                    vector.tensor_copy(out=QI[:, hs[hf]], in_=QT1[:, hs[hf]])
                for hf in range(2):
                    vector.reciprocal(out=QY[:, hs[hf]], in_=QC[:, hs[hf]])
                for hf in range(2):
                    vector.tensor_copy(out=QT1[:, hs[hf]], in_=QI[:, hs[hf]])
                for hf in range(2):
                    vector.tensor_tensor(out=QT1[:, hs[hf]], in0=QT1[:, hs[hf]],
                                         in1=QY[:, hs[hf]], op=ALU.mult)
                if xd_guard is not None:
                    vector.wait_ge(sPE, xd_guard)
                last = None
                for hf in range(2):
                    last = vector.tensor_tensor(
                        out=xdst[:, hs[hf]].rearrange("p (b k) -> p b k", k=32),
                        in0=QT1[:, hs[hf]].rearrange("p (b k) -> p b k", k=32),
                        in1=SCL[:, bs[hf]].unsqueeze(2).broadcast_to(
                            [128, 16, 32]),
                        op=ALU.mult)
                last.then_inc(sDVE, 1)

            if not do_p2:
                return
            for i in range(8):
                vector.wait_ge(sWL, wl_marks[("fw", i)])
                quant(FWT[:], FWQ[:], act_wdq_sq(i),
                      act_wdq_sq(i - 1) if i > 0 else None,
                      pe_marks[("wdtr", i - 1, 1)] if i > 0 else
                      pe_marks[("atr", H - 1, NQB - 1, 1)])

            hs2 = [slice(0, 512), slice(512, 1024)]
            for tt in range(NTT):
                vector.wait_ge(sIN, in_marks[("x2", tt)])
                if tt == 0:
                    vector.wait_ge(sPE, pe_marks[("wdtr", 7, 1)])
                for hf in range(2):
                    s = hs2[hf]
                    vector.scalar_tensor_tensor(out=X2B[:, s],
                                                in0=XT[tt % 2][:, s],
                                                scalar=2.0, in1=BOB[:, s],
                                                op0=ALU.mult, op1=ALU.add)
                vector.wait_ge(sPE, pe_marks[("z", tt)])
                if tt >= 2:
                    vector.wait_ge(sOUT, 16 * (tt - 1))
                last = None
                for hf in range(2):
                    s = hs2[hf]
                    last = vector.tensor_tensor(out=YO[tt % 2][:, s],
                                                in0=X2B[:, s],
                                                in1=PS_SQ[0][:, s], op=ALU.add)
                last.then_inc(sDVE, 1)
                if tt >= 1:
                    vector.wait_ge(sACT, act_ln3sum(tt - 1))
                vector.tensor_reduce(out=RED2[:, 0:1],
                                     in_=YO[tt % 2][:, 0:512].rearrange(
                                         "p (b k) -> p b k", k=512),
                                     axis=AX.X, op=ALU.add)
                vector.tensor_reduce(out=RED2[:, 1:2],
                                     in_=YO[tt % 2][:, 512:1024].rearrange(
                                         "p (b k) -> p b k", k=512),
                                     axis=AX.X, op=ALU.add).then_inc(sDVE, 1)
                vector.wait_ge(sACT, act_x1(tt))
                quant(X1T[tt % 2], XD[:], act_xqsq(tt),
                      act_xqsq(tt - 1) if tt > 0 else act_wdq_sq(7),
                      pe_marks[("xdtr", tt - 1, 1)] if tt > 0 else
                      pe_marks[("wdtr", 7, 1)])
                for hf in range(2):
                    s = hs2[hf]
                    vector.tensor_tensor(out=X2B[:, s], in0=X1T[tt % 2][:, s],
                                         in1=FB[:, s], op=ALU.add)
                vector.wait_ge(sPE, pe_marks[("ffn", tt)])
                last = None
                for hf in range(2):
                    s = hs2[hf]
                    last = vector.tensor_tensor(out=YO[tt % 2][:, s],
                                                in0=X2B[:, s],
                                                in1=PS_SQ[1][:, s], op=ALU.add)
                last.then_inc(sDVE, 1)
                vector.wait_ge(sACT, act_ln2sum(tt))
                vector.tensor_reduce(out=RED2[:, 0:1],
                                     in_=YO[tt % 2][:, 0:512].rearrange(
                                         "p (b k) -> p b k", k=512),
                                     axis=AX.X, op=ALU.add)
                vector.tensor_reduce(out=RED2[:, 1:2],
                                     in_=YO[tt % 2][:, 512:1024].rearrange(
                                         "p (b k) -> p b k", k=512),
                                     axis=AX.X, op=ALU.add).then_inc(sDVE, 1)

        _apply_swaps(nc)
    return nc


def kernel(L0_attn_X, L0_attn_W_in, L0_attn_b_in, L0_attn_W_out, L0_attn_b_out,
           L0_ffn_weight, L0_ffn_bias):
    X = np.asarray(L0_attn_X, dtype=np.float32)
    W_in = np.asarray(L0_attn_W_in, dtype=np.float32)
    b_in = np.asarray(L0_attn_b_in, dtype=np.float32)
    W_out = np.asarray(L0_attn_W_out, dtype=np.float32)
    b_out = np.asarray(L0_attn_b_out, dtype=np.float32)
    fw = np.asarray(L0_ffn_weight, dtype=np.float32)
    fb = np.asarray(L0_ffn_bias, dtype=np.float32)

    winT = np.ascontiguousarray(
        W_in.reshape(H, 256, 8, 128).transpose(0, 3, 2, 1).reshape(H * 128,
                                                                   2048))
    b3 = b_in.reshape(H, 4, DH)
    binq = np.ascontiguousarray(b3[:, 0, :].T)
    bink = np.ascontiguousarray(b3[:, 1, :].T)
    binvg = np.ascontiguousarray(b_in.reshape(H, 256)[:, 128:].T)
    woutT = np.ascontiguousarray(
        W_out.reshape(D, 8, 128).transpose(2, 1, 0).reshape(128, 8 * D))
    bout_bc = np.ascontiguousarray(np.broadcast_to(b_out, (128, D)))
    fb_bc = np.ascontiguousarray(np.broadcast_to(fb, (128, D)))
    idn = np.eye(128, dtype=np.float32)

    nc = build_nc()
    in_maps = []
    for b in range(NCORES):
        in_maps.append({
            "x": np.ascontiguousarray(X[b]),
            "winT": winT, "binq": binq, "bink": bink, "binvg": binvg,
            "woutT": woutT, "bout": bout_bc, "fw": fw, "fb": fb_bc,
            "idn": idn,
        })
    res = run_bass_kernel_spmd(nc, in_maps, core_ids=list(range(NCORES)))
    out = np.stack([r["y"] for r in res.results], axis=0)
    return out


if __name__ == "__main__":
    rng = np.random.default_rng(0)
    args = dict(
        L0_attn_X=rng.normal(size=(NCORES, T, D)).astype(np.float32),
        L0_attn_W_in=(rng.normal(size=(4 * D, D)) / 32).astype(np.float32),
        L0_attn_b_in=(rng.normal(size=(4 * D,)) * 0.1).astype(np.float32),
        L0_attn_W_out=(rng.normal(size=(D, D)) / 32).astype(np.float32),
        L0_attn_b_out=(rng.normal(size=(D,)) * 0.1).astype(np.float32),
        L0_ffn_weight=(rng.normal(size=(D, D)) * 0.02).astype(np.float32),
        L0_ffn_bias=(rng.normal(size=(D,)) * 0.02).astype(np.float32),
    )
    y = kernel(**args)
    print(y.shape, y.dtype, float(np.abs(y).mean()))


# version 10 (final)
# speedup vs baseline: 1.4512x; 1.0000x over previous
import numpy as np
from contextlib import ExitStack

import concourse.bass as bass
import concourse.mybir as mybir
from concourse.bass_utils import run_bass_kernel_spmd

F32 = mybir.dt.float32
BF16 = mybir.dt.bfloat16
AF = mybir.ActivationFunctionType
ALU = mybir.AluOpType
AX = mybir.AxisListType

T = 4096
D = 1024
H = 16
DH = 64
NCORES = 8
NTT = T // 128
TQ = 1024
NQB = T // TQ
NKC = T // 128
EPS = 1e-4
EPS_LN = 1e-5
CLAMP = 12000.0
M1 = 1.5 * (2.0 ** 20)
DBG_Q2OFF = False

_swap_to = {}


def _rs(bi):
    _swap_to[bi.ins.name] = AF.Rsqrt
    return bi


def _apply_swaps(nc):
    for blk in nc.m.functions[0].blocks:
        insts = blk.instructions
        for i, inst in enumerate(insts):
            if inst.opcode == "Activation" and inst.name in _swap_to:
                clone = mybir.InstActivation(
                    name=inst.name, ins=list(inst.ins), outs=list(inst.outs),
                    func=_swap_to[inst.name])
                clone.merge_dependencies_from(inst)
                clone.sync_info = inst.sync_info
                clone.engine = inst.engine
                insts[i] = clone


ACT_P0 = 6 * NTT
ACT_PRO = 64
ACT_QB = NKC + 3
ACT_HEAD = ACT_PRO + NQB * ACT_QB
ACT_P2B = ACT_P0 + H * ACT_HEAD
ACT_TTB = ACT_P2B + 24

DVE_P0 = NTT
DVE_HEAD = NQB * (NKC + 2)
DVE_P2B = DVE_P0 + H * DVE_HEAD
DVE_TTB = DVE_P2B + 16

PE_P0 = 2 * NTT
PE_PRO = 56
PE_QB = 2 * NKC + 4
PE_HEAD = PE_PRO + NQB * PE_QB
PE_P2B = PE_P0 + H * PE_HEAD


def act_probase(h):
    return ACT_P0 + h * ACT_HEAD


def act_qbbase(h, qb):
    return act_probase(h) + ACT_PRO + qb * ACT_QB


def act_sig(h, qb, kc):
    return act_qbbase(h, qb) + kc + 1


def act_aevac(h, qb):
    return act_qbbase(h, qb) + NKC + 1


def act_ofevac(h, qb, half):
    return act_qbbase(h, qb) + NKC + 2 + half


def f_qev(nb):
    return 1 + nb


def f_kev(nb):
    return 9 + nb


def f_vgev(nb):
    return 17 + nb if nb < 4 else 21 + nb


def f_vgtrev(g):
    return 21 + g if g < 4 else 25 + g


def f_qsq(q):
    return 33 + 3 * q


def f_q2ev(nb):
    return 34 + 3 * (nb // 2) + (nb % 2)


def f_ksq(q):
    return 45 + 5 * q


def f_k2hi(nb):
    return 46 + 5 * (nb // 2) + 2 * (nb % 2)


def f_k2lo(nb):
    return 47 + 5 * (nb // 2) + 2 * (nb % 2)


def act_wdq_sq(i):
    return ACT_P2B + 3 * i + 1


def act_wdq_ev(i, half):
    return ACT_P2B + 3 * i + 2 + half


def act_ln2sq(tt):
    return ACT_TTB + 11 * tt + 1


def act_ln2sum(tt):
    return ACT_TTB + 11 * tt + 2


def act_x1(tt):
    return ACT_TTB + 11 * tt + 4


def act_xqsq(tt):
    return ACT_TTB + 11 * tt + 5


def act_xdf(tt, half):
    return ACT_TTB + 11 * tt + 6 + half


def act_ln3sq(tt):
    return ACT_TTB + 11 * tt + 8


def act_ln3sum(tt):
    return ACT_TTB + 11 * tt + 9


def act_y(tt):
    return ACT_TTB + 11 * tt + 11


def dve_headbase(h):
    return DVE_P0 + h * DVE_HEAD


def dve_z(h, qb, kc):
    return dve_headbase(h) + qb * (NKC + 2) + kc + 1


def dve_otm(h, qb, half):
    return dve_headbase(h) + qb * (NKC + 2) + NKC + half + 1


def dve_wdq_y(i):
    return DVE_P2B + 2 * i + 1


def dve_wdq_xd(i):
    return DVE_P2B + 2 * i + 2


def dve_r1(tt):
    return DVE_TTB + 6 * tt + 1


def dve_red2(tt):
    return DVE_TTB + 6 * tt + 2


def dve_x1q_y(tt):
    return DVE_TTB + 6 * tt + 3


def dve_x1q_xd(tt):
    return DVE_TTB + 6 * tt + 4


def dve_r2(tt):
    return DVE_TTB + 6 * tt + 5


def dve_red3(tt):
    return DVE_TTB + 6 * tt + 6


def tile_of(gi):
    return gi // (NQB * NKC), (gi // NKC) % NQB, gi % NKC


def build_nc(stage="full"):
    if stage.startswith("h") and stage[1:].isdigit():
        do_heads = int(stage[1:])
        do_attn = True
        do_p2 = False
    else:
        do_heads = {"p0": 0, "pro": 1, "attn1": 1, "noph2": H, "full": H}[stage]
        do_attn = stage in ("attn1", "noph2", "full")
        do_p2 = stage == "full"
    _swap_to.clear()
    nc = bass.Bass("TRN2", target_bir_lowering=False,
                   dynamic_dma_scratch_size=6144)

    x_h = nc.dram_tensor("x", [T, D], F32, kind="ExternalInput").ap()
    winT_h = nc.dram_tensor("winT", [H * 128, 8 * 256], F32,
                            kind="ExternalInput").ap()
    binq_h = nc.dram_tensor("binq", [64, H], F32, kind="ExternalInput").ap()
    bink_h = nc.dram_tensor("bink", [64, H], F32, kind="ExternalInput").ap()
    binvg_h = nc.dram_tensor("binvg", [128, H], F32, kind="ExternalInput").ap()
    woutT_h = nc.dram_tensor("woutT", [128, 8 * D], F32,
                             kind="ExternalInput").ap()
    bout_h = nc.dram_tensor("bout", [128, D], F32, kind="ExternalInput").ap()
    fw_h = nc.dram_tensor("fw", [D, D], F32, kind="ExternalInput").ap()
    fb_h = nc.dram_tensor("fb", [128, D], F32, kind="ExternalInput").ap()
    idn_h = nc.dram_tensor("idn", [128, 128], F32, kind="ExternalInput").ap()
    y_h = nc.dram_tensor("y", [T, D], F32, kind="ExternalOutput").ap()
    hN = stage.startswith("h") and stage[1:].isdigit()
    dbg = {}
    if stage != "full":
        dbg["xnf"] = nc.dram_tensor("dbg_xnf", [128, 8 * T], BF16,
                                    kind="ExternalOutput").ap()
        if stage in ("pro", "attn1", "noph2") or hN:
            dbg["qaug"] = nc.dram_tensor("dbg_qaug", [97, T], BF16,
                                         kind="ExternalOutput").ap()
            dbg["kaug"] = nc.dram_tensor("dbg_kaug", [97, T], BF16,
                                         kind="ExternalOutput").ap()
            dbg["vaug"] = nc.dram_tensor("dbg_vaug", [128, NKC * 65], BF16,
                                         kind="ExternalOutput").ap()
            dbg["gtm"] = nc.dram_tensor("dbg_gtm", [128, 2048], F32,
                                        kind="ExternalOutput").ap()
        if stage in ("attn1", "noph2") or hN:
            dbg["of"] = nc.dram_tensor("dbg_of", [128, 8 * T], BF16,
                                       kind="ExternalOutput").ap()
            dbg["vgf"] = nc.dram_tensor("dbg_vgf", [128, 2048], F32,
                                        kind="ExternalOutput").ap()
            dbg["otm"] = nc.dram_tensor("dbg_otm", [128, 1024], F32,
                                        kind="ExternalOutput").ap()
            for hh in range(do_heads):
                dbg[f"vaug{hh}"] = nc.dram_tensor(f"dbg_vaug{hh}",
                                                  [128, NKC * 65], BF16,
                                                  kind="ExternalOutput").ap()
            for nm, shp, dt in [("wt0", [128, TQ], BF16), ("wt1", [128, TQ], BF16),
                                ("atm", [128, TQ], F32), ("zt0", [128, TQ], F32),
                                ("zt1", [128, TQ], F32), ("fwq", [128, D], F32),
                                ("sig0", [128, TQ], F32), ("sig1", [128, TQ], F32)]:
                dbg[nm] = nc.dram_tensor("dbg_" + nm, shp, dt,
                                         kind="ExternalOutput").ap()

    ctx = ExitStack()
    sb = lambda name, shape, dt: ctx.enter_context(nc.sbuf_tensor(name, shape, dt))
    pst = lambda name, shape: ctx.enter_context(nc.psum_tensor(name, shape, F32))
    sem = lambda name: ctx.enter_context(nc.semaphore(name))

    with ctx:
        IDN = sb("IDN", [128, 128], F32)
        BQ = sb("BQ", [64, H], F32)
        BQm2 = sb("BQm2", [64, H], F32)
        BK = sb("BK", [64, H], F32)
        BVG = sb("BVG", [128, H], F32)
        BOB = sb("BOB", [128, D], F32)
        FB = sb("FB", [128, D], F32)
        ONESK = sb("ONESK", [64, 97], BF16)
        ONESQ = ONESK[:, 0:65]
        NEGL = sb("NEGL", [65, 97], BF16)
        STAT = sb("STAT", [128, 16], F32)
        RED2 = sb("RED2", [128, 2], F32)
        RCOL = sb("RCOL", [128, 8], F32)
        EPSC = sb("EPSC", [128, 1], F32)
        BM = sb("BM", [128, 32], F32)
        SCL = sb("SCL", [128, 32], F32)
        RSC = sb("RSC", [128, 32], F32)

        XT = [sb(f"XT{i}", [128, D], F32) for i in range(2)]
        XNF = sb("XNF", [128, 8 * T], BF16)
        OF = sb("OF", [128, 8 * T], BF16)
        WITB = sb("WITB", [128, 8 * 256], BF16)
        QAUG = sb("QAUG", [97, T], BF16)
        KAUG = sb("KAUG", [97, T], BF16)
        VGF = sb("VGF", [128, 2048], F32)
        VAUG = sb("VAUG", [128, NKC * 65], BF16)
        GTM = sb("GTM", [128, 2048], F32)

        SIG = [sb(f"SIG{i}", [128, TQ], F32) for i in range(2)]
        ATm = sb("ATm", [128, TQ], F32)
        SIG.append(ATm)
        ZT = [sb(f"ZT{i}", [128, TQ], F32) for i in range(2)]
        WTl = [sb(f"WTb{i}", [128, TQ], BF16) for i in range(2)]
        WTl.append(XT[0][:].bitcast(BF16))
        ZT.append(XT[1])

        def wtl_ap(i, a=0, b=TQ):
            return WTl[i][:, a:b] if i < 2 else WTl[2][:, a:b]
        OTM = sb("OTM", [128, 1024], F32)
        FWQ = sb("FWQ", [128, D], F32)
        X1T = [VGF[:, 0:D], VGF[:, D:2 * D]]
        YO = [GTM[:, 0:D], GTM[:, D:2 * D]]

        XNB = ZT[0]
        LNSa = XNB
        SQS = WTl[0][0:64, :]
        WOT = XNF[:, 0:8 * D]
        WDT = XNF[:, 8 * D:16 * D]
        XDF = XNF[:, 16 * D:17 * D]
        FWT = OTM
        AEV = FWQ[0:65, :]
        X2B = FWQ
        LNS2 = ATm
        QI = ATm[:].bitcast(mybir.dt.int32)
        QY = SIG[0]
        QC = SIG[1]
        QT1 = ZT[0]
        XD = ZT[1]

        PS_SQ = [pst("PS_SQ0", [128, TQ]), pst("PS_SQ1", [128, TQ])]
        PS_A = pst("PS_A", [97, TQ])
        PS_TR = [pst("PS_TR0", [128, 512]), pst("PS_TR1", [128, 512])]

        sIN = sem("sIN")
        sWL = sem("sWL")
        sOUT = sem("sOUT")
        sPE = sem("sPE")
        sACT = sem("sACT")
        sDVE = sem("sDVE")
        sQ2 = sem("sQ2")
        sQM = sem("sQM")

        in_marks = {}
        in_cnt = [0]

        def in_mark(key):
            in_cnt[0] += 16
            in_marks[key] = in_cnt[0]

        wl_marks = {}
        wl_cnt = [0]

        def wl_mark(key):
            wl_cnt[0] += 16
            wl_marks[key] = wl_cnt[0]

        pe_marks = {}
        pe_cnt = [0]

        def pe_mark(key):
            pe_cnt[0] += 1
            pe_marks[key] = pe_cnt[0]

        block = ctx.enter_context(nc.Block())

        @block.sync
        def _(sync):
            for ap_sb, ap_h in [(IDN[:], idn_h), (BQ[:], binq_h), (BK[:], bink_h),
                                (BVG[:], binvg_h), (BOB[:], bout_h), (FB[:], fb_h)]:
                sync.dma_start(out=ap_sb, in_=ap_h).then_inc(sIN, 16)
                in_cnt[0] += 16
            in_marks["consts"] = in_cnt[0]
            for i in range(NTT):
                if i >= 2:
                    sync.wait_ge(sACT, 6 * (i - 2) + 4)
                    sync.wait_ge(sDVE, i - 1)
                sync.dma_start(out=XT[i % 2][:],
                               in_=x_h[i * 128:(i + 1) * 128, :]).then_inc(sIN, 16)
                in_mark(("x0", i))
            for h in range(do_heads):
                sync.wait_ge(sACT, act_probase(h) + f_q2ev(7))
                sync.dma_start(out=QAUG[66:67, :] if DBG_Q2OFF else QAUG[65:66, :],
                               in_=QAUG[64:65, :]).then_inc(sQ2, 16)
                if f"vaug{h}" in dbg:
                    sync.wait_ge(sACT, act_probase(h) + ACT_PRO)
                    sync.dma_start(out=dbg[f"vaug{h}"], in_=VAUG[:]
                                   ).then_inc(sOUT, 16)
            if do_p2:
                sync.wait_ge(sPE, PE_P2B)
                sync.wait_ge(sWL, 16 * (H + 1))
                for i in range(8):
                    if i >= 1:
                        sync.wait_ge(sDVE, dve_wdq_y(i - 1))
                    sync.dma_start(out=FWT[:],
                                   in_=fw_h[128 * i:128 * (i + 1), :]
                                   ).then_inc(sWL, 16)
                    wl_marks[("fw", i)] = 16 * (H + 1) + 16 * (i + 1)
                sync.wait_ge(sACT, ACT_P0)
                for tt in range(NTT):
                    if tt >= 2:
                        sync.wait_ge(sDVE, dve_r1(tt - 2))
                    sync.dma_start(out=XT[tt % 2][:],
                                   in_=x_h[tt * 128:(tt + 1) * 128, :]
                                   ).then_inc(sIN, 16)
                    in_mark(("x2", tt))
                    if tt >= 1:
                        sync.wait_ge(sACT, act_y(tt - 1))
                        sync.dma_start(out=y_h[(tt - 1) * 128:tt * 128, :],
                                       in_=YO[(tt - 1) % 2][:]).then_inc(sOUT, 16)
                sync.wait_ge(sACT, act_y(NTT - 1))
                sync.dma_start(out=y_h[(NTT - 1) * 128:NTT * 128, :],
                               in_=YO[(NTT - 1) % 2][:]).then_inc(sOUT, 16)
                sync.wait_ge(sOUT, 16 * NTT)
            if stage != "full":
                if stage == "p0":
                    sync.wait_ge(sACT, ACT_P0)
                elif stage == "pro":
                    sync.wait_ge(sACT, act_probase(0) + ACT_PRO)
                else:
                    sync.wait_ge(sACT, act_probase(do_heads - 1) + ACT_HEAD)
                nst = 16 * sum(1 for k in dbg if k.startswith("vaug") and k != "vaug")
                for key, hbm in dbg.items():
                    if key.startswith("vaug") and key != "vaug":
                        continue
                    src = {"xnf": XNF[:], "qaug": QAUG[:], "kaug": KAUG[:],
                           "vaug": VAUG[:], "gtm": GTM[:], "of": OF[:],
                           "vgf": VGF[:], "otm": OTM[:], "wt0": WTl[0][:],
                           "wt1": WTl[1][:], "atm": ATm[:], "zt0": ZT[0][:],
                           "zt1": ZT[1][:], "fwq": FWQ[:], "sig0": SIG[0][:],
                           "sig1": SIG[1][:]}[key]
                    sync.dma_start(out=hbm, in_=src).then_inc(sOUT, 16)
                    nst += 16
                sync.dma_start(out=y_h[0:128, :], in_=XT[0][:]).then_inc(sOUT, 16)
                nst += 16
                sync.wait_ge(sOUT, nst)

        @block.gpsimd
        def _(gpsimd):
            for h in range(do_heads):
                if h >= 1:
                    gpsimd.wait_ge(sPE, PE_P0 + (h - 1) * PE_HEAD + 28)
                gpsimd.dma_start(out=WITB[:],
                                 in_=winT_h[128 * h:128 * (h + 1), :]
                                 ).then_inc(sWL, 16)
                wl_mark(("wit", h))
            if not do_p2:
                return
            gpsimd.wait_ge(sPE, PE_P0 + (H - 1) * PE_HEAD + 28)
            gpsimd.dma_start(out=WOT, in_=woutT_h).then_inc(sWL, 16)
            wl_mark("wot")

        def emit_wv(tensor, lh, lqb, lkc, ab):
            gi = (lh * NQB + lqb) * NKC + lkc
            tensor.wait_ge(sDVE, dve_z(lh, lqb, lkc))
            if lqb == 0:
                tensor.wait_ge(sACT, ab + f_vgtrev(lkc // 4))
            if lkc == 0 and (lh * NQB + lqb) > 0:
                pq = lh * NQB + lqb - 1
                tensor.wait_ge(sACT, act_aevac(pq // NQB, pq % NQB))
            for half in range(2):
                last = tensor.matmul(PS_A[0:65, half * 512:(half + 1) * 512],
                                     VAUG[:, lkc * 65:(lkc + 1) * 65],
                                     wtl_ap(gi % 3, half * 512,
                                            (half + 1) * 512),
                                     start=(lkc == 0), stop=(lkc == NKC - 1))
            last.then_inc(sPE, 1)
            pe_mark(("wv", lh, lqb, lkc))

        @block.tensor
        def _(tensor):
            tensor.wait_ge(sIN, in_marks["consts"])
            for i in range(NTT):
                tensor.wait_ge(sACT, 6 * i + 4)
                for half in range(2):
                    if i > 0:
                        tensor.wait_ge(sACT, 6 * (i - 1) + 5 + half)
                    bank = PS_TR[half]
                    for j in range(4):
                        c = half * 4 + j
                        last = tensor.transpose(bank[:, j * 128:(j + 1) * 128],
                                                XNB[:, c * 128:(c + 1) * 128],
                                                IDN[:])
                    last.then_inc(sPE, 1)
                    pe_mark(("xntr", i, half))

            for h in range(do_heads):
                ab = act_probase(h)
                tensor.wait_ge(sWL, wl_marks[("wit", h)])
                if h == 0:
                    tensor.wait_ge(sACT, ACT_P0)
                for nb in range(8):
                    bank = PS_SQ[nb % 2]
                    if h > 0 and nb < 2:
                        tensor.wait_ge(sACT, act_sig(h - 1, NQB - 1, NKC - 1))
                    if nb >= 2:
                        tensor.wait_ge(sACT, ab + f_qev(nb - 2))
                    for c in range(8):
                        last = tensor.matmul(
                            bank[0:64, 0:512],
                            WITB[:, c * 256:c * 256 + 64],
                            XNF[:, c * T + nb * 512:c * T + (nb + 1) * 512],
                            start=(c == 0), stop=(c == 7))
                    last.then_inc(sPE, 1)
                    pe_mark(("qproj", h, nb))
                for nb in range(8):
                    bank = PS_SQ[nb % 2]
                    tensor.wait_ge(sACT,
                                   ab + (f_qev(6 + nb) if nb < 2 else f_kev(nb - 2)))
                    for c in range(8):
                        last = tensor.matmul(
                            bank[0:64, 0:512],
                            WITB[:, c * 256 + 64:c * 256 + 128],
                            XNF[:, c * T + nb * 512:c * T + (nb + 1) * 512],
                            start=(c == 0), stop=(c == 7))
                    last.then_inc(sPE, 1)
                    pe_mark(("kproj", h, nb))
                for nb in range(4):
                    bank = PS_SQ[nb % 2]
                    tensor.wait_ge(sACT,
                                   ab + (f_kev(6 + nb) if nb < 2 else f_vgev(nb - 2)))
                    for c in range(8):
                        last = tensor.matmul(
                            bank[:, 0:512],
                            WITB[:, c * 256 + 128:c * 256 + 256],
                            XNF[:, c * T + nb * 512:c * T + (nb + 1) * 512],
                            start=(c == 0), stop=(c == 7))
                    last.then_inc(sPE, 1)
                    pe_mark(("vgproj", h, nb))
                for g in range(4):
                    bank = PS_TR[g % 2]
                    tensor.wait_ge(sACT, ab + f_vgev(g))
                    if g >= 2:
                        tensor.wait_ge(sACT, ab + f_vgtrev(g - 2))
                    elif h > 0:
                        tensor.wait_ge(sACT, act_ofevac(h - 1, NQB - 1, g))
                    for j in range(4):
                        kc = g * 4 + j
                        last = tensor.transpose(
                            bank[:, j * 128:(j + 1) * 128],
                            VGF[:, (kc % 16) * 128:(kc % 16 + 1) * 128],
                            IDN[:])
                    last.then_inc(sPE, 1)
                    pe_mark(("vgtr", h, g))
                for nb in range(4, 8):
                    bank = PS_SQ[nb % 2]
                    tensor.wait_ge(sACT, ab + f_vgev(nb - 2))
                    for c in range(8):
                        last = tensor.matmul(
                            bank[:, 0:512],
                            WITB[:, c * 256 + 128:c * 256 + 256],
                            XNF[:, c * T + nb * 512:c * T + (nb + 1) * 512],
                            start=(c == 0), stop=(c == 7))
                    last.then_inc(sPE, 1)
                    pe_mark(("vgproj", h, nb))
                for g in range(4, 8):
                    bank = PS_TR[g % 2]
                    tensor.wait_ge(sACT, ab + f_vgev(g))
                    tensor.wait_ge(sACT, ab + f_vgtrev(g - 2))
                    for j in range(4):
                        kc = g * 4 + j
                        last = tensor.transpose(
                            bank[:, j * 128:(j + 1) * 128],
                            VGF[:, (kc % 16) * 128:(kc % 16 + 1) * 128],
                            IDN[:])
                    last.then_inc(sPE, 1)
                    pe_mark(("vgtr", h, g))
                for nb in range(8):
                    tensor.wait_ge(sACT, ab + f_qsq(nb // 2))
                    if nb == 0 and h > 0:
                        tensor.wait_ge(sACT, act_aevac(h - 1, NQB - 1))
                    if nb >= 1:
                        tensor.wait_ge(sACT, ab + f_q2ev(nb - 1))
                    tensor.matmul(PS_A[0:65, 0:512], ONESQ,
                                  SQS[:, (nb % 2) * 512:(nb % 2 + 1) * 512],
                                  start=True, stop=True).then_inc(sPE, 1)
                    pe_mark(("q2", h, nb))
                for nb in range(8):
                    tensor.wait_ge(sACT, ab + f_ksq(nb // 2))
                    if nb == 0:
                        tensor.wait_ge(sACT, ab + f_q2ev(7))
                    if nb >= 1:
                        tensor.wait_ge(sACT, ab + f_k2lo(nb - 1))
                    tensor.matmul(PS_A[0:97, 0:512], ONESK[:],
                                  SQS[:, (nb % 2) * 512:(nb % 2 + 1) * 512],
                                  start=True, stop=False).then_inc(sPE, 1)
                    pe_mark(("k2", h, nb))
                    tensor.wait_ge(sACT, ab + f_k2hi(nb))
                    tensor.matmul(PS_A[0:97, 0:512], NEGL[64:65, :],
                                  KAUG[64:65, nb * 512:(nb + 1) * 512],
                                  start=False, stop=True).then_inc(sPE, 1)
                    pe_mark(("k2b", h, nb))

                for qb in range(NQB if do_attn else 0):
                    for kc in range(NKC):
                        gi = (h * NQB + qb) * NKC + kc
                        buf = gi % 2
                        bank = PS_SQ[buf]
                        if qb == 0 and kc < 2:
                            tensor.wait_ge(sACT, ab + f_vgev(6 + kc))
                        else:
                            ph, pqb, pkc = tile_of(gi - 2)
                            tensor.wait_ge(sACT, act_sig(ph, pqb, pkc))
                        if qb == 0 and kc == 0:
                            tensor.wait_ge(sACT, ab + f_k2lo(7))
                            tensor.wait_ge(sQM, h + 1)
                        for half in range(2):
                            last = tensor.matmul(
                                bank[:, half * 512:(half + 1) * 512],
                                KAUG[:, kc * 128:(kc + 1) * 128],
                                QAUG[:, qb * TQ + half * 512:qb * TQ + half * 512 + 512],
                                start=True, stop=True)
                        last.then_inc(sPE, 1)
                        pe_mark(("sq", h, qb, kc))
                        if kc >= 1:
                            emit_wv(tensor, h, qb, kc - 1, ab)
                    emit_wv(tensor, h, qb, NKC - 1, ab)
                    tensor.wait_ge(sACT, act_aevac(h, qb))
                    for half in range(2):
                        bank = PS_TR[half]
                        if qb == 0:
                            tensor.wait_ge(sACT, ab + f_vgtrev(6 + half))
                        else:
                            tensor.wait_ge(sACT, act_ofevac(h, qb - 1, half))
                        for j in range(4):
                            g = half * 4 + j
                            last = tensor.transpose(
                                bank[:, j * 130:j * 130 + 65],
                                AEV[:, g * 128:(g + 1) * 128],
                                IDN[0:65, 0:65])
                        last.then_inc(sPE, 1)
                        pe_mark(("atr", h, qb, half))
                    for half in range(2):
                        bank = PS_TR[half]
                        tensor.wait_ge(sDVE, dve_otm(h, qb, half))
                        for j in range(4):
                            g = half * 4 + j
                            last = tensor.transpose(
                                bank[:, j * 128:(j + 1) * 128],
                                OTM[:, g * 128:(g + 1) * 128], IDN[:])
                        last.then_inc(sPE, 1)
                        pe_mark(("otr", h, qb, half))

            if not do_p2:
                return
            for i in range(8):
                for half in range(2):
                    bank = PS_TR[half]
                    tensor.wait_ge(sDVE, dve_wdq_xd(i))
                    if i == 0:
                        tensor.wait_ge(sACT, act_ofevac(H - 1, NQB - 1, half))
                    else:
                        tensor.wait_ge(sACT, act_wdq_ev(i - 1, half))
                    for j in range(4):
                        c = half * 4 + j
                        last = tensor.transpose(
                            bank[:, j * 128:(j + 1) * 128],
                            FWQ[:, c * 128:(c + 1) * 128], IDN[:])
                    last.then_inc(sPE, 1)
                    pe_mark(("wdtr", i, half))
            tensor.wait_ge(sWL, wl_marks["wot"])
            tensor.wait_ge(sACT, ACT_P2B)
            for tt in range(NTT):
                if tt >= 1:
                    tensor.wait_ge(sDVE, dve_r1(tt - 1))
                for c in range(8):
                    for half in range(2):
                        last = tensor.matmul(
                            PS_SQ[0][:, half * 512:(half + 1) * 512],
                            OF[:, c * T + tt * 128:c * T + (tt + 1) * 128],
                            WOT[:, c * D + half * 512:c * D + (half + 1) * 512],
                            start=(c == 0), stop=(c == 7))
                last.then_inc(sPE, 1)
                pe_mark(("z", tt))
                for half in range(2):
                    bank = PS_TR[half]
                    tensor.wait_ge(sDVE, dve_x1q_xd(tt))
                    if tt == 0:
                        tensor.wait_ge(sACT, act_wdq_ev(7, half))
                    else:
                        tensor.wait_ge(sACT, act_xdf(tt - 1, half))
                    for j in range(4):
                        c = half * 4 + j
                        last = tensor.transpose(
                            bank[:, j * 128:(j + 1) * 128],
                            XD[:, c * 128:(c + 1) * 128], IDN[:])
                    last.then_inc(sPE, 1)
                    pe_mark(("xdtr", tt, half))
                tensor.wait_ge(sACT, act_xdf(tt, 1))
                if tt >= 1:
                    tensor.wait_ge(sDVE, dve_r2(tt - 1))
                for c in range(8):
                    for half in range(2):
                        last = tensor.matmul(
                            PS_SQ[1][:, half * 512:(half + 1) * 512],
                            XDF[:, c * 128:(c + 1) * 128],
                            WDT[:, c * D + half * 512:c * D + (half + 1) * 512],
                            start=(c == 0), stop=(c == 7))
                last.then_inc(sPE, 1)
                pe_mark(("ffn", tt))

        def ln_stats(scalar, red2):
            scalar.activation(STAT[:, 0:1], red2[:, 0:1], AF.Prelu, alpha=1.0,
                              bias=red2[:, 1:2], scale=1.0).then_inc(sACT, 1)
            scalar.activation(STAT[:, 7:8], STAT[:, 1:2], AF.Prelu, alpha=1.0,
                              scale=1.0 / D)
            scalar.activation(STAT[:, 6:7], STAT[:, 0:1], AF.Prelu, alpha=1.0,
                              scale=-1.0 / D)
            scalar.activation(STAT[:, 0:1], STAT[:, 0:1], AF.Prelu, alpha=1.0,
                              scale=1.0 / D)
            scalar.activation(STAT[:, 5:6], STAT[:, 0:1], AF.Prelu, alpha=1.0,
                              scale=STAT[:, 6:7])
            scalar.activation(STAT[:, 2:3], STAT[:, 5:6], AF.Prelu, alpha=1.0,
                              bias=STAT[:, 7:8], scale=1.0)
            _rs(scalar.activation(STAT[:, 3:4], STAT[:, 2:3], AF.Square,
                                  bias=EPSC[:, 0:1], scale=1.0))
            scalar.activation(STAT[:, 4:5], STAT[:, 3:4], AF.Prelu, alpha=1.0,
                              scale=STAT[:, 6:7])
            scalar.activation(STAT[:, 8:9], EPSC[:, 0:1], AF.Prelu, alpha=1.0,
                              scale=1.0).then_inc(sACT, 1)

        @block.scalar
        def _(scalar):
            for i in range(NTT):
                scalar.wait_ge(sIN, in_marks[("x0", i)])
                if i >= 1:
                    scalar.wait_ge(sPE, pe_marks[("xntr", i - 1, 1)])
                xt = XT[i % 2]
                scalar.activation(LNSa[:], xt[:], AF.Square,
                                  accum_out=STAT[:, 1:2]).then_inc(sACT, 1)
                scalar.wait_ge(sDVE, i + 1)
                ln_stats(scalar, RED2)
                scalar.activation(XNB[:], xt[:], AF.Prelu, alpha=1.0,
                                  bias=STAT[:, 4:5], scale=STAT[:, 3:4]
                                  ).then_inc(sACT, 1)
                for half in range(2):
                    scalar.wait_ge(sPE, pe_marks[("xntr", i, half)])
                    src = PS_TR[half][:].rearrange("p (j f) -> p j f", f=128)
                    dst = XNF[:].rearrange("p (c f) -> p c f", f=T)[
                        :, half * 4:(half + 1) * 4, i * 128:(i + 1) * 128]
                    scalar.activation(dst, src, AF.Copy).then_inc(sACT, 1)

            for h in range(do_heads):
                ab = act_probase(h)
                if h == 0:
                    scalar.wait_ge(sDVE, DVE_P0)
                for nb in range(8):
                    scalar.wait_ge(sPE, pe_marks[("qproj", h, nb)])
                    bank = PS_SQ[nb % 2]
                    scalar.activation(QAUG[0:64, nb * 512:(nb + 1) * 512],
                                      bank[0:64, 0:512], AF.Prelu, alpha=1.0,
                                      bias=BQm2[:, h:h + 1], scale=-2.0
                                      ).then_inc(sACT, 1)
                for nb in range(8):
                    scalar.wait_ge(sPE, pe_marks[("kproj", h, nb)])
                    bank = PS_SQ[nb % 2]
                    scalar.activation(KAUG[0:64, nb * 512:(nb + 1) * 512],
                                      bank[0:64, 0:512], AF.Prelu, alpha=1.0,
                                      bias=BK[:, h:h + 1], scale=1.0
                                      ).then_inc(sACT, 1)
                for nb in range(8):
                    scalar.wait_ge(sPE, pe_marks[("vgproj", h, nb)])
                    if nb >= 4:
                        scalar.wait_ge(sPE, pe_marks[("vgtr", h, nb - 4)])
                    bank = PS_SQ[nb % 2]
                    scalar.activation(VGF[:, (nb % 4) * 512:(nb % 4 + 1) * 512],
                                      bank[:, 0:512], AF.Prelu, alpha=1.0,
                                      bias=BVG[:, h:h + 1], scale=1.0
                                      ).then_inc(sACT, 1)
                    if nb == 3:
                        for g in range(4):
                            scalar.wait_ge(sPE, pe_marks[("vgtr", h, g)])
                            if g < 2 and h > 0:
                                scalar.wait_ge(
                                    sPE, pe_marks[("wv", h - 1, NQB - 1, NKC - 1)])
                                scalar.wait_ge(sDVE, dve_otm(h - 1, NQB - 1, 1))
                            bank2 = PS_TR[g % 2]
                            src = bank2[:].rearrange("p (j f) -> p j f", f=128)
                            vdst = VAUG[:].rearrange("p (k f) -> p k f", f=65)[
                                :, g * 4:(g + 1) * 4, 0:64]
                            scalar.activation(vdst, src[:, :, 0:64], AF.Copy)
                            gdst = GTM[:].rearrange("p (k f) -> p k f", f=DH)[
                                :, g * 4:(g + 1) * 4, :]
                            scalar.activation(gdst, src[:, :, 64:128], AF.Copy
                                              ).then_inc(sACT, 1)
                for g in range(4, 8):
                    scalar.wait_ge(sPE, pe_marks[("vgtr", h, g)])
                    bank2 = PS_TR[g % 2]
                    src = bank2[:].rearrange("p (j f) -> p j f", f=128)
                    vdst = VAUG[:].rearrange("p (k f) -> p k f", f=65)[
                        :, g * 4:(g + 1) * 4, 0:64]
                    scalar.activation(vdst, src[:, :, 0:64], AF.Copy)
                    gdst = GTM[:].rearrange("p (k f) -> p k f", f=DH)[
                        :, g * 4:(g + 1) * 4, :]
                    scalar.activation(gdst, src[:, :, 64:128], AF.Copy
                                      ).then_inc(sACT, 1)
                for q in range(4):
                    if q == 0:
                        if h > 0:
                            lg = h * 128 - 1
                            while lg % 3:
                                lg -= 1
                            scalar.wait_ge(sPE, pe_marks[("wv", *tile_of(lg))])
                        else:
                            scalar.wait_ge(sPE, pe_marks[("vgtr", h, 7)])
                    else:
                        scalar.wait_ge(sPE, pe_marks[("q2", h, 2 * q - 1)])
                    scalar.activation(SQS[:], QAUG[0:64, q * 1024:(q + 1) * 1024],
                                      AF.Square, scale=0.5).then_inc(sACT, 1)
                    for half in range(2):
                        nb = 2 * q + half
                        scalar.wait_ge(sPE, pe_marks[("q2", h, nb)])
                        if q == 0 and half == 0 and h > 0:
                            scalar.wait_ge(sQ2, 16 * h)
                        scalar.activation(QAUG[64:65, nb * 512:(nb + 1) * 512],
                                          PS_A[64:65, 0:512], AF.Copy
                                          ).then_inc(sACT, 1)
                for q in range(4):
                    scalar.wait_ge(sPE, pe_marks[("q2", h, 7)] if q == 0
                                   else pe_marks[("k2b", h, 2 * q - 1)])
                    scalar.activation(SQS[:], KAUG[0:64, q * 1024:(q + 1) * 1024],
                                      AF.Square, scale=1.0).then_inc(sACT, 1)
                    for half in range(2):
                        nb = 2 * q + half
                        scalar.wait_ge(sPE, pe_marks[("k2", h, nb)])
                        scalar.activation(KAUG[64:65, nb * 512:(nb + 1) * 512],
                                          PS_A[64:65, 0:512], AF.Copy
                                          ).then_inc(sACT, 1)
                        scalar.wait_ge(sPE, pe_marks[("k2b", h, nb)])
                        scalar.activation(KAUG[96:97, nb * 512:(nb + 1) * 512],
                                          PS_A[96:97, 0:512], AF.Copy
                                          ).then_inc(sACT, 1)

                for qb in range(NQB if do_attn else 0):
                    for kc in range(NKC):
                        gi = (h * NQB + qb) * NKC + kc
                        scalar.wait_ge(sPE, pe_marks[("sq", h, qb, kc)])
                        if gi >= 3:
                            ph, pqb, pkc = tile_of(gi - 3)
                            scalar.wait_ge(sDVE, dve_z(ph, pqb, pkc))
                        _rs(scalar.activation(SIG[gi % 3][:], PS_SQ[gi % 2][:],
                                              AF.Square)).then_inc(sACT, 1)
                    scalar.wait_ge(sPE, pe_marks[("wv", h, qb, NKC - 1)])
                    if h * NQB + qb > 0:
                        pq = h * NQB + qb - 1
                        scalar.wait_ge(sPE, pe_marks[("atr", pq // NQB, pq % NQB, 1)])
                    scalar.activation(AEV[:], PS_A[0:65, :], AF.Copy).then_inc(sACT, 1)
                    for half in range(2):
                        scalar.wait_ge(sPE, pe_marks[("otr", h, qb, half)])
                        bank = PS_TR[half]
                        part = (h % 2) * 64
                        base = (h // 2) * T + qb * TQ + half * 512
                        src = bank[part:part + 64, :].rearrange(
                            "p (g f) -> p g f", f=128)
                        dst = OF[part:part + 64, base:base + 512].rearrange(
                            "p (g f) -> p g f", f=128)
                        scalar.activation(dst, src, AF.Copy).then_inc(sACT, 1)

            if not do_p2:
                return
            for i in range(8):
                scalar.wait_ge(sDVE, dve_wdq_y(i))
                scalar.activation(LNS2[:], QY[:], AF.Square).then_inc(sACT, 1)
                for half in range(2):
                    scalar.wait_ge(sPE, pe_marks[("wdtr", i, half)])
                    bank = PS_TR[half]
                    src = bank[:].rearrange("p (j f) -> p j f", f=128)
                    dst = WDT.rearrange("p (c f) -> p c f", f=D)[
                        :, half * 4:(half + 1) * 4, i * 128:(i + 1) * 128]
                    scalar.activation(dst, src, AF.Copy).then_inc(sACT, 1)
            for tt in range(NTT):
                R = YO[tt % 2]
                scalar.wait_ge(sDVE, dve_r1(tt))
                scalar.activation(LNS2[:], R[:], AF.Square,
                                  accum_out=STAT[:, 1:2]).then_inc(sACT, 1)
                scalar.wait_ge(sDVE, dve_red2(tt))
                ln_stats(scalar, RED2)
                scalar.activation(X1T[tt % 2][:], R[:], AF.Prelu, alpha=1.0,
                                  bias=STAT[:, 4:5], scale=STAT[:, 3:4]
                                  ).then_inc(sACT, 1)
                scalar.wait_ge(sDVE, dve_x1q_y(tt))
                scalar.activation(LNS2[:], QY[:], AF.Square).then_inc(sACT, 1)
                for half in range(2):
                    scalar.wait_ge(sPE, pe_marks[("xdtr", tt, half)])
                    bank = PS_TR[half]
                    src = bank[:].rearrange("p (j f) -> p j f", f=128)
                    dst = XDF.rearrange("p (c f) -> p c f", f=128)[
                        :, half * 4:(half + 1) * 4, :]
                    scalar.activation(dst, src, AF.Copy).then_inc(sACT, 1)
                scalar.wait_ge(sDVE, dve_r2(tt))
                scalar.activation(LNS2[:], R[:], AF.Square,
                                  accum_out=STAT[:, 1:2]).then_inc(sACT, 1)
                scalar.wait_ge(sDVE, dve_red3(tt))
                ln_stats(scalar, RED2)
                scalar.activation(R[:], R[:], AF.Prelu, alpha=1.0, bias=STAT[:, 4:5],
                                  scale=STAT[:, 3:4]).then_inc(sACT, 1)

        @block.vector
        def _(vector):
            vector.wait_ge(sIN, in_marks["consts"])
            vector.tensor_scalar(out=BQm2[:], in0=BQ[:], scalar1=-2.0,
                                 scalar2=None, op0=ALU.mult)
            vector.memset(KAUG[64:96, :], 0.0)
            vector.memset(KAUG[64:66, :], 1.0)
            vector.memset(KAUG[96:97, :], 0.0)
            vector.memset(QAUG[64:96, :], 1.0)
            vector.memset(QAUG[96:97, :], 1.0)
            vector.memset(VAUG[:].rearrange("p (k f) -> p k f", f=65)[:, :, 64:65],
                          1.0)
            vector.memset(EPSC[:], EPS_LN)
            vector.memset(ONESK[:], 0.0)
            vector.memset(ONESK[:, 64:65], 1.0)
            vector.memset(ONESK[:, 96:97], 1.0)
            vector.memset(NEGL[64:65, :], 0.0)
            vector.memset(NEGL[64:65, 96:97], -1.0)

            for i in range(NTT):
                vector.wait_ge(sIN, in_marks[("x0", i)])
                if i >= 1:
                    vector.wait_ge(sACT, 6 * (i - 1) + 2)
                xt = XT[i % 2]
                vector.tensor_reduce(out=RED2[:],
                                     in_=xt[:].rearrange("p (b k) -> p b k",
                                                         k=512),
                                     axis=AX.X, op=ALU.add).then_inc(sDVE, 1)

            for h in range(do_heads):
                ab = act_probase(h)
                if do_attn:
                    vector.wait_ge(sQ2, 16 * (h + 1))
                    vector.memset(QAUG[64:65, :], 1.0).then_inc(sQM, 1)
                for qb in range(NQB if do_attn else 0):
                    for kc in range(NKC):
                        gi = (h * NQB + qb) * NKC + kc
                        buf = gi % 2
                        vector.wait_ge(sACT, act_sig(h, qb, kc))
                        for hf in range(2):
                            sl = slice(hf * 512, (hf + 1) * 512)
                            vector.tensor_scalar(out=ZT[gi % 3][:, sl],
                                                 in0=SIG[gi % 3][:, sl],
                                                 scalar1=CLAMP,
                                                 scalar2=-EPS * 100.0,
                                                 op0=ALU.min, op1=ALU.mult)
                        for hf in range(2):
                            sl = slice(hf * 512, (hf + 1) * 512)
                            vector.scalar_tensor_tensor(
                                out=ZT[gi % 3][:, sl], in0=ZT[gi % 3][:, sl],
                                scalar=100.0, in1=ZT[gi % 3][:, sl],
                                op0=ALU.add, op1=ALU.mult)
                        if gi >= 3:
                            ph, pqb, pkc = tile_of(gi - 3)
                            vector.wait_ge(sPE, pe_marks[("wv", ph, pqb, pkc)])
                        last = None
                        for hf in range(2):
                            sl = slice(hf * 512, (hf + 1) * 512)
                            last = vector.tensor_tensor(
                                out=wtl_ap(gi % 3, hf * 512, (hf + 1) * 512),
                                in0=ZT[gi % 3][:, sl], in1=ZT[gi % 3][:, sl],
                                op=ALU.mult)
                        last.then_inc(sDVE, 1)
                    for half in range(2):
                        vector.wait_ge(sPE, pe_marks[("atr", h, qb, half)])
                        bank = PS_TR[half]
                        for j in range(4):
                            vector.tensor_scalar(
                                out=RCOL[:, j:j + 1],
                                in0=bank[:, j * 130 + 64:j * 130 + 65],
                                scalar1=EPS, scalar2=None, op0=ALU.add)
                        for j in range(4):
                            vector.reciprocal(out=RCOL[:, 4 + j:5 + j],
                                              in_=RCOL[:, j:j + 1])
                        last = None
                        for j in range(4):
                            g = half * 4 + j
                            at = bank[:, j * 130:j * 130 + 65]
                            kcg = qb * 8 + g
                            col = g * 128 + (h % 2) * 64
                            last = vector.scalar_tensor_tensor(
                                out=OTM[:, col:col + 64],
                                in0=at[:, 0:64], scalar=RCOL[:, 4 + j:5 + j],
                                in1=GTM[:, kcg * DH:(kcg + 1) * DH],
                                op0=ALU.mult, op1=ALU.mult)
                        last.then_inc(sDVE, 1)

            def quant(xsrc, xdst, gate_sq, gate_prev_sq, xd_guard):
                hs = [slice(0, 512), slice(512, 1024)]
                bs = [slice(0, 16), slice(16, 32)]
                x3h = [xsrc[:, s].rearrange("p (b k) -> p b k", k=32) for s in hs]
                for hf in range(2):
                    vector.tensor_reduce(out=BM[:, bs[hf]], in_=x3h[hf],
                                         axis=AX.X, op=ALU.max,
                                         apply_absolute_value=True)
                for hf in range(2):
                    vector.tensor_scalar(out=SCL[:, bs[hf]], in0=BM[:, bs[hf]],
                                         scalar1=1.0 / 7.5, scalar2=1e-30,
                                         op0=ALU.mult, op1=ALU.max)
                for hf in range(2):
                    vector.reciprocal(out=RSC[:, bs[hf]], in_=SCL[:, bs[hf]])
                if gate_prev_sq is not None:
                    vector.wait_ge(sACT, gate_prev_sq)
                last = None
                for hf in range(2):
                    last = vector.tensor_tensor(
                        out=QY[:, hs[hf]].rearrange("p (b k) -> p b k", k=32),
                        in0=x3h[hf],
                        in1=RSC[:, bs[hf]].unsqueeze(2).broadcast_to(
                            [128, 16, 32]),
                        op=ALU.mult)
                last.then_inc(sDVE, 1)
                vector.wait_ge(sACT, gate_sq)
                for hf in range(2):
                    vector.tensor_scalar(out=QT1[:, hs[hf]], in0=LNS2[:, hs[hf]],
                                         scalar1=4.0, scalar2=None,
                                         op0=ALU.is_ge)
                for hf in range(2):
                    vector.tensor_scalar(out=QC[:, hs[hf]], in0=LNS2[:, hs[hf]],
                                         scalar1=16.0, scalar2=None,
                                         op0=ALU.is_ge)
                for hf in range(2):
                    vector.tensor_scalar(out=QT1[:, hs[hf]], in0=QT1[:, hs[hf]],
                                         scalar1=-0.5, scalar2=1.0,
                                         op0=ALU.mult, op1=ALU.add)
                for hf in range(2):
                    vector.tensor_scalar(out=QC[:, hs[hf]], in0=QC[:, hs[hf]],
                                         scalar1=-0.5, scalar2=1.0,
                                         op0=ALU.mult, op1=ALU.add)
                for hf in range(2):
                    vector.scalar_tensor_tensor(out=QC[:, hs[hf]],
                                                in0=QC[:, hs[hf]], scalar=8.0,
                                                in1=QT1[:, hs[hf]],
                                                op0=ALU.mult, op1=ALU.mult)
                for hf in range(2):
                    vector.tensor_tensor(out=QT1[:, hs[hf]], in0=QY[:, hs[hf]],
                                         in1=QC[:, hs[hf]], op=ALU.mult)
                for hf in range(2):
                    vector.tensor_copy(out=QI[:, hs[hf]], in_=QT1[:, hs[hf]])
                for hf in range(2):
                    vector.reciprocal(out=QY[:, hs[hf]], in_=QC[:, hs[hf]])
                for hf in range(2):
                    vector.tensor_copy(out=QT1[:, hs[hf]], in_=QI[:, hs[hf]])
                for hf in range(2):
                    vector.tensor_tensor(out=QT1[:, hs[hf]], in0=QT1[:, hs[hf]],
                                         in1=QY[:, hs[hf]], op=ALU.mult)
                if xd_guard is not None:
                    vector.wait_ge(sPE, xd_guard)
                last = None
                for hf in range(2):
                    last = vector.tensor_tensor(
                        out=xdst[:, hs[hf]].rearrange("p (b k) -> p b k", k=32),
                        in0=QT1[:, hs[hf]].rearrange("p (b k) -> p b k", k=32),
                        in1=SCL[:, bs[hf]].unsqueeze(2).broadcast_to(
                            [128, 16, 32]),
                        op=ALU.mult)
                last.then_inc(sDVE, 1)

            if not do_p2:
                return
            for i in range(8):
                vector.wait_ge(sWL, wl_marks[("fw", i)])
                quant(FWT[:], FWQ[:], act_wdq_sq(i),
                      act_wdq_sq(i - 1) if i > 0 else None,
                      pe_marks[("wdtr", i - 1, 1)] if i > 0 else
                      pe_marks[("atr", H - 1, NQB - 1, 1)])

            hs2 = [slice(0, 512), slice(512, 1024)]
            for tt in range(NTT):
                vector.wait_ge(sIN, in_marks[("x2", tt)])
                if tt == 0:
                    vector.wait_ge(sPE, pe_marks[("wdtr", 7, 1)])
                for hf in range(2):
                    s = hs2[hf]
                    vector.scalar_tensor_tensor(out=X2B[:, s],
                                                in0=XT[tt % 2][:, s],
                                                scalar=2.0, in1=BOB[:, s],
                                                op0=ALU.mult, op1=ALU.add)
                vector.wait_ge(sPE, pe_marks[("z", tt)])
                if tt >= 2:
                    vector.wait_ge(sOUT, 16 * (tt - 1))
                last = None
                for hf in range(2):
                    s = hs2[hf]
                    last = vector.tensor_tensor(out=YO[tt % 2][:, s],
                                                in0=X2B[:, s],
                                                in1=PS_SQ[0][:, s], op=ALU.add)
                last.then_inc(sDVE, 1)
                if tt >= 1:
                    vector.wait_ge(sACT, act_ln3sum(tt - 1))
                vector.tensor_reduce(out=RED2[:, 0:1],
                                     in_=YO[tt % 2][:, 0:512].rearrange(
                                         "p (b k) -> p b k", k=512),
                                     axis=AX.X, op=ALU.add)
                vector.tensor_reduce(out=RED2[:, 1:2],
                                     in_=YO[tt % 2][:, 512:1024].rearrange(
                                         "p (b k) -> p b k", k=512),
                                     axis=AX.X, op=ALU.add).then_inc(sDVE, 1)
                vector.wait_ge(sACT, act_x1(tt))
                quant(X1T[tt % 2], XD[:], act_xqsq(tt),
                      act_xqsq(tt - 1) if tt > 0 else act_wdq_sq(7),
                      pe_marks[("xdtr", tt - 1, 1)] if tt > 0 else
                      pe_marks[("wdtr", 7, 1)])
                for hf in range(2):
                    s = hs2[hf]
                    vector.tensor_tensor(out=X2B[:, s], in0=X1T[tt % 2][:, s],
                                         in1=FB[:, s], op=ALU.add)
                vector.wait_ge(sPE, pe_marks[("ffn", tt)])
                last = None
                for hf in range(2):
                    s = hs2[hf]
                    last = vector.tensor_tensor(out=YO[tt % 2][:, s],
                                                in0=X2B[:, s],
                                                in1=PS_SQ[1][:, s], op=ALU.add)
                last.then_inc(sDVE, 1)
                vector.wait_ge(sACT, act_ln2sum(tt))
                vector.tensor_reduce(out=RED2[:, 0:1],
                                     in_=YO[tt % 2][:, 0:512].rearrange(
                                         "p (b k) -> p b k", k=512),
                                     axis=AX.X, op=ALU.add)
                vector.tensor_reduce(out=RED2[:, 1:2],
                                     in_=YO[tt % 2][:, 512:1024].rearrange(
                                         "p (b k) -> p b k", k=512),
                                     axis=AX.X, op=ALU.add).then_inc(sDVE, 1)

        _apply_swaps(nc)
    return nc


def kernel(L0_attn_X, L0_attn_W_in, L0_attn_b_in, L0_attn_W_out, L0_attn_b_out,
           L0_ffn_weight, L0_ffn_bias):
    X = np.asarray(L0_attn_X, dtype=np.float32)
    W_in = np.asarray(L0_attn_W_in, dtype=np.float32)
    b_in = np.asarray(L0_attn_b_in, dtype=np.float32)
    W_out = np.asarray(L0_attn_W_out, dtype=np.float32)
    b_out = np.asarray(L0_attn_b_out, dtype=np.float32)
    fw = np.asarray(L0_ffn_weight, dtype=np.float32)
    fb = np.asarray(L0_ffn_bias, dtype=np.float32)

    winT = np.ascontiguousarray(
        W_in.reshape(H, 256, 8, 128).transpose(0, 3, 2, 1).reshape(H * 128,
                                                                   2048))
    b3 = b_in.reshape(H, 4, DH)
    binq = np.ascontiguousarray(b3[:, 0, :].T)
    bink = np.ascontiguousarray(b3[:, 1, :].T)
    binvg = np.ascontiguousarray(b_in.reshape(H, 256)[:, 128:].T)
    woutT = np.ascontiguousarray(
        W_out.reshape(D, 8, 128).transpose(2, 1, 0).reshape(128, 8 * D))
    bout_bc = np.ascontiguousarray(np.broadcast_to(b_out, (128, D)))
    fb_bc = np.ascontiguousarray(np.broadcast_to(fb, (128, D)))
    idn = np.eye(128, dtype=np.float32)

    nc = build_nc()
    in_maps = []
    for b in range(NCORES):
        in_maps.append({
            "x": np.ascontiguousarray(X[b]),
            "winT": winT, "binq": binq, "bink": bink, "binvg": binvg,
            "woutT": woutT, "bout": bout_bc, "fw": fw, "fb": fb_bc,
            "idn": idn,
        })
    res = run_bass_kernel_spmd(nc, in_maps, core_ids=list(range(NCORES)))
    out = np.stack([r["y"] for r in res.results], axis=0)
    return out


if __name__ == "__main__":
    rng = np.random.default_rng(0)
    args = dict(
        L0_attn_X=rng.normal(size=(NCORES, T, D)).astype(np.float32),
        L0_attn_W_in=(rng.normal(size=(4 * D, D)) / 32).astype(np.float32),
        L0_attn_b_in=(rng.normal(size=(4 * D,)) * 0.1).astype(np.float32),
        L0_attn_W_out=(rng.normal(size=(D, D)) / 32).astype(np.float32),
        L0_attn_b_out=(rng.normal(size=(D,)) * 0.1).astype(np.float32),
        L0_ffn_weight=(rng.normal(size=(D, D)) * 0.02).astype(np.float32),
        L0_ffn_bias=(rng.normal(size=(D,)) * 0.02).astype(np.float32),
    )
    y = kernel(**args)
    print(y.shape, y.dtype, float(np.abs(y).mean()))
